# version 13
# baseline (speedup 1.0000x reference)
import numpy as np
import ml_dtypes
from contextlib import ExitStack

import concourse.bass as bass
import concourse.mybir as mybir
from concourse.bass_utils import run_bass_kernel_spmd

F32 = mybir.dt.float32
BF16 = mybir.dt.bfloat16
AF = mybir.ActivationFunctionType
ALU = mybir.AluOpType
AXX = mybir.AxisListType.X

S = 8192
DM = 2048
NH = 16
HD = 128
TPC = 1024
NSLOT = 8
NKT = 64
INTER = 5632
NIB = INTER // 128
NHT = DM // 128
NHC = DM // 128
EPS = 1e-5
SCALE = 1.0 / float(np.sqrt(HD))
SHIFT = 12.0
N_CORES = 8


def slot_qtile(s, c):
    return 8 * (7 - s) + c


def slot_count(s):
    return 64 - 8 * s


def n_active(kt):
    return 8 - kt // 8


class Plan:
    ENGINES = ("sync", "gpsimd", "tensor", "scalar", "vector")

    def __init__(self):
        self.ops = {e: [] for e in self.ENGINES}
        self.counts = {}

    def inc(self, name, k=1):
        self.counts[name] = self.counts.get(name, 0) + k
        return self.counts[name]

    def val(self, name):
        return self.counts.get(name, 0)

    def emit(self, engine, fn):
        self.ops[engine].append(fn)

    def wait(self, engine, name, value):
        if value <= 0:
            return
        self.emit(engine, lambda e, sems, name=name, value=value: e.wait_ge(sems[name], value))

    def op_inc(self, engine, name, k, fn):
        v = self.inc(name, k)
        self.emit(
            engine,
            lambda e, sems, fn=fn, name=name, k=k: fn(e).then_inc(sems[name], k),
        )
        return v

    def op(self, engine, fn):
        self.emit(engine, lambda e, sems, fn=fn: fn(e))


def build_program(debug=False, dump_xno=False):
    nc = bass.Bass("TRN2", target_bir_lowering=False)

    x_full = nc.dram_tensor("x_full", [S, DM], F32, kind="ExternalInput").ap()
    x_own = nc.dram_tensor("x_own", [TPC, DM], F32, kind="ExternalInput").ap()
    masks_in = nc.dram_tensor("masks", [NKT * 128, 128], BF16, kind="ExternalInput").ap()
    wg_in = nc.dram_tensor("wg", [INTER, DM], F32, kind="ExternalInput").ap()
    wu_in = nc.dram_tensor("wu", [INTER, DM], F32, kind="ExternalInput").ap()
    wd_in = nc.dram_tensor("wd", [DM, INTER], F32, kind="ExternalInput").ap()
    gs_in = nc.dram_tensor("gs", [NIB, 128], F32, kind="ExternalInput").ap()
    us_in = nc.dram_tensor("us", [NIB, 128], F32, kind="ExternalInput").ap()
    ds_in = nc.dram_tensor("ds", [NHT, 128], F32, kind="ExternalInput").ap()
    idb_in = nc.dram_tensor("idb", [128, 128], BF16, kind="ExternalInput").ap()
    idf_in = nc.dram_tensor("idf", [128, 128], F32, kind="ExternalInput").ap()
    out = nc.dram_tensor("out", [TPC, DM], F32, kind="ExternalOutput").ap()

    xn = nc.dram_tensor("xn", [S, DM], BF16, kind="Internal").ap()
    _g1 = nc.dram_tensor("_g1", [TPC, DM], BF16, kind="Internal").ap()
    xn_own = nc.dram_tensor("xn_own", [TPC, DM], BF16, kind="Internal").ap()
    _g2 = nc.dram_tensor("_g2", [TPC, DM], BF16, kind="Internal").ap()
    r1 = nc.dram_tensor("r1", [TPC, DM], F32, kind="Internal").ap()
    if dump_xno:
        dbg_xno2 = nc.dram_tensor("dbg_xno2", [TPC, DM], F32, kind="ExternalOutput").ap()
    if debug:
        dbg_xn = nc.dram_tensor("dbg_xn", [S, DM], F32, kind="ExternalOutput").ap()
        dbg_r1 = nc.dram_tensor("dbg_r1", [TPC, DM], F32, kind="ExternalOutput").ap()
        dbg_osb = nc.dram_tensor("dbg_osb", [NH * 128, 1024], BF16, kind="ExternalOutput").ap()
        dbg_nr = nc.dram_tensor("dbg_nr", [NH, 1024], F32, kind="ExternalOutput").ap()
        dbg_rcol = nc.dram_tensor("dbg_rcol", [NH * 128, 8], F32, kind="ExternalOutput").ap()
        dbg_xno = nc.dram_tensor("dbg_xno", [TPC, DM], F32, kind="ExternalOutput").ap()
        dbg_qt = nc.dram_tensor("dbg_qt", [NH * 128, TPC], BF16, kind="ExternalOutput").ap()

    xf_t = x_full.rearrange("(t p) c -> p t c", p=128)
    xo_t = x_own.rearrange("(t p) c -> p t c", p=128)
    xn_t = xn.rearrange("(t p) c -> p t c", p=128)
    xno_t = xn_own.rearrange("(t p) c -> p t c", p=128)
    xn_kv = xn.rearrange("(kt r) c -> r kt c", r=128)
    r1_t = r1.rearrange("(t p) c -> p t c", p=128)
    out_t = out.rearrange("(t p) c -> p t c", p=128)
    masks_t = masks_in.rearrange("(kt r) c -> r kt c", r=128)

    plan = Plan()
    stk = ExitStack()

    def sb(name, shape, dt):
        return stk.enter_context(nc.sbuf_tensor(name, shape, dt))

    sems = {}
    for s_ in ("ld_s", "ld_g", "st_s", "st_g", "st_a", "pe", "act", "dve", "gp"):
        sems[s_] = stk.enter_context(nc.semaphore(s_))

    ident_b = sb("ident_b", [128, 128], BF16)
    ident_f = sb("ident_f", [128, 128], F32)
    ones_b = sb("ones_b", [128, 1], BF16)
    ones_f = sb("ones_f", [128, 1], F32)
    eps_sb = sb("eps_sb", [128, 1], F32)
    gate_s_sb = sb("gate_s_sb", [128, NIB], F32)
    up_s_sb = sb("up_s_sb", [128, NIB], F32)
    down_s_sb = sb("down_s_sb", [128, NHT], F32)

    with stk:
        plan.op_inc("sync", "ld_s", 16, lambda e: e.dma_start(out=ident_b[:], in_=idb_in[:, :]))
        plan.op_inc("sync", "ld_s", 16, lambda e: e.dma_start(out=ident_f[:], in_=idf_in[:, :]))
        gs_nat = sb("gs_nat", [NIB, 128], F32)
        us_nat = sb("us_nat", [NIB, 128], F32)
        ds_nat = sb("ds_nat", [NHT, 128], F32)
        plan.op_inc("gpsimd", "ld_g", 16, lambda e: e.dma_start(out=gs_nat[:], in_=gs_in[:, :]))
        plan.op_inc("gpsimd", "ld_g", 16, lambda e: e.dma_start(out=us_nat[:], in_=us_in[:, :]))
        plan.op_inc("gpsimd", "ld_g", 16, lambda e: e.dma_start(out=ds_nat[:], in_=ds_in[:, :]))
        plan.op("gpsimd", lambda e: e.memset(ones_b[:], 1.0))
        plan.op("gpsimd", lambda e: e.memset(ones_f[:], 1.0))
        plan.op_inc("gpsimd", "gp", 1, lambda e: e.memset(eps_sb[:], EPS))
        ld_s_consts = plan.val("ld_s")
        ld_g_consts = plan.val("ld_g")
        gp_eps = plan.val("gp")

        with ExitStack() as pc:
            ps_sc = pc.enter_context(nc.psum_tensor("ps_sc", [128, 64], F32))
            plan.wait("tensor", "ld_s", ld_s_consts)
            plan.wait("tensor", "ld_g", ld_g_consts)
            for nat, dst, n in (
                (gs_nat, gate_s_sb, NIB),
                (us_nat, up_s_sb, NIB),
                (ds_nat, down_s_sb, NHT),
            ):
                pv = plan.val("dve")
                plan.wait("tensor", "dve", pv)
                plan.op_inc(
                    "tensor", "pe", 1,
                    lambda e, nat=nat, n=n: e.matmul(
                        ps_sc[:, 0:n], nat[:, :], ident_f[0:n, 0:n], start=True, stop=True
                    ),
                )
                plan.wait("vector", "pe", plan.val("pe"))
                plan.op_inc(
                    "vector", "dve", 1,
                    lambda e, dst=dst, n=n: e.tensor_copy(dst[:, 0:n], ps_sc[:, 0:n]),
                )
            dve_scales = plan.val("dve")

        NG0 = 36
        with ExitStack() as p0:
            NB0 = 3
            xg = [p0.enter_context(nc.sbuf_tensor(f"xg{b}", [128, 2 * DM], F32)) for b in range(NB0)]
            xn_st = [p0.enter_context(nc.sbuf_tensor(f"xnst{b}", [128, 2 * DM], BF16)) for b in range(NB0)]
            sq_scr = p0.enter_context(nc.sbuf_tensor("sq_scr", [128, DM], BF16))
            sums = p0.enter_context(nc.sbuf_tensor("sums", [128, 2 * NG0], F32))
            sumsq = p0.enter_context(nc.sbuf_tensor("sumsq", [128, 2 * NG0], F32))
            nmean = p0.enter_context(nc.sbuf_tensor("nmean", [128, 2 * NG0], F32))
            msq = p0.enter_context(nc.sbuf_tensor("msq", [128, 2 * NG0], F32))
            nmq = p0.enter_context(nc.sbuf_tensor("nmq", [128, 2 * NG0], F32))
            varp = p0.enter_context(nc.sbuf_tensor("varp", [128, 2 * NG0], F32))
            lnv = p0.enter_context(nc.sbuf_tensor("lnv", [128, 2 * NG0], F32))
            rstd = p0.enter_context(nc.sbuf_tensor("rstd", [128, 2 * NG0], F32))
            nmr = p0.enter_context(nc.sbuf_tensor("nmr", [128, 2 * NG0], F32))

            g_ld = [0] * NG0
            g_stats = [0] * NG0
            g_nrm = [0] * NG0
            g_st = [0] * NG0

            for g in range(NG0):
                b = g % NB0
                own = g >= 32
                t0 = 2 * g if not own else 2 * (g - 32)
                src = xf_t if not own else xo_t
                ldq = "sync" if g % 2 == 0 else "gpsimd"
                ldsem = "ld_s" if g % 2 == 0 else "ld_g"
                if g >= NB0:
                    plan.wait(ldq, "act", g_stats[g - NB0])
                    plan.wait(ldq, "dve", g_nrm[g - NB0])
                g_ld[g] = (ldsem, plan.op_inc(
                    ldq, ldsem, 16,
                    lambda e, b=b, src=src, t0=t0: e.dma_start(
                        out=xg[b].rearrange("p (t c) -> p t c", t=2),
                        in_=src[:, t0 : t0 + 2, :],
                    ),
                ))
                plan.wait("scalar", g_ld[g][0], g_ld[g][1])
                if g == 0:
                    plan.wait("scalar", "gp", gp_eps)
                for i in range(2):
                    plan.op(
                        "scalar",
                        lambda e, b=b, i=i, g=g: e.activation(
                            sq_scr[:], xg[b][:, i * DM : (i + 1) * DM], AF.Square,
                            accum_out=sumsq[:, 2 * g + i : 2 * g + i + 1],
                        ),
                    )
                for i in range(2):
                    plan.op(
                        "scalar",
                        lambda e, b=b, i=i, g=g: e.activation(
                            sq_scr[:], xg[b][:, i * DM : (i + 1) * DM], AF.Copy,
                            accum_out=sums[:, 2 * g + i : 2 * g + i + 1],
                        ),
                    )
                cc = slice(2 * g, 2 * g + 2)
                plan.op("scalar", lambda e, cc=cc: e.activation(nmean[:, cc], sums[:, cc], AF.Copy, scale=-1.0 / DM))
                a_msq = plan.op_inc(
                    "scalar", "act", 1,
                    lambda e, cc=cc: e.activation(msq[:, cc], nmean[:, cc], AF.Square),
                )
                plan.wait("vector", "act", a_msq)
                d_nmq = plan.op_inc(
                    "vector", "dve", 1,
                    lambda e, cc=cc: e.tensor_scalar(nmq[:, cc], msq[:, cc], -1.0, EPS, ALU.mult, ALU.add),
                )
                plan.wait("scalar", "dve", d_nmq)
                plan.op("scalar", lambda e, cc=cc: e.activation(varp[:, cc], sumsq[:, cc], AF.Copy, scale=1.0 / DM))
                for i in range(2):
                    ci = slice(2 * g + i, 2 * g + i + 1)
                    plan.op("scalar", lambda e, ci=ci: e.activation(lnv[:, ci], varp[:, ci], AF.Ln, bias=nmq[:, ci]))
                    fn = lambda e, ci=ci: e.activation(rstd[:, ci], lnv[:, ci], AF.Exp, scale=-0.5)
                    if i == 1:
                        g_stats[g] = plan.op_inc("scalar", "act", 1, fn)
                    else:
                        plan.op("scalar", fn)
                plan.wait("vector", "act", g_stats[g])
                if g >= NB0:
                    plan.wait("vector", "st_a", g_st[g - NB0])
                plan.op(
                    "vector",
                    lambda e, b=b, g=g: e.tensor_scalar(
                        xn_st[b][:, 0:DM], xg[b][:, 0:DM],
                        nmean[:, 2 * g : 2 * g + 1], rstd[:, 2 * g : 2 * g + 1],
                        ALU.add, ALU.mult,
                    ),
                )
                g_nrm[g] = plan.op_inc(
                    "vector", "dve", 1,
                    lambda e, b=b, g=g: e.tensor_scalar(
                        xn_st[b][:, DM : 2 * DM], xg[b][:, DM : 2 * DM],
                        nmean[:, 2 * g + 1 : 2 * g + 2], rstd[:, 2 * g + 1 : 2 * g + 2],
                        ALU.add, ALU.mult,
                    ),
                )
                def emit_store(gq):
                    bq = gq % NB0
                    ownq = gq >= 32
                    tq = 2 * gq if not ownq else 2 * (gq - 32)
                    dstt2 = xn_t if not ownq else xno_t
                    plan.wait("scalar", "dve", g_nrm[gq])
                    g_st[gq] = plan.op_inc(
                        "scalar", "st_a", 16,
                        lambda e, bq=bq, dstt2=dstt2, tq=tq: e.dma_start(
                            out=dstt2[:, tq : tq + 2, :],
                            in_=xn_st[bq].rearrange("p (t c) -> p t c", t=2),
                        ),
                    )
                if g >= 1:
                    emit_store(g - 1)
                if g == NG0 - 1:
                    emit_store(g)
            st_phase0 = plan.val("st_a")

            if debug:
                plan.wait("gpsimd", "st_a", st_phase0)
                plan.op_inc(
                    "gpsimd", "st_g", 16,
                    lambda e: e.dma_start(out=dbg_xn[:, :], in_=xn[:, :]),
                )
                dbg0 = plan.op_inc(
                    "gpsimd", "st_g", 16,
                    lambda e: e.dma_start(out=dbg_xno[:, :], in_=xn_own[:, :]),
                )
                plan.wait("sync", "st_g", dbg0)

        with ExitStack() as pa:
            mask_sb = pa.enter_context(nc.sbuf_tensor("mask_sb", [128, NKT * 128], BF16))
            XT = [pa.enter_context(nc.sbuf_tensor(f"XT{b}", [128, S], BF16)) for b in range(2)]
            QTall = pa.enter_context(nc.sbuf_tensor("QTall", [128, NH * TPC], BF16))
            xoin = pa.enter_context(nc.sbuf_tensor("xoin", [128, NSLOT * DM], BF16))
            VT = [pa.enter_context(nc.sbuf_tensor(f"VT{b}", [128, NKT * 128], BF16)) for b in range(2)]
            XO = [pa.enter_context(nc.sbuf_tensor(f"XO{b}", [128, NSLOT * 128], F32)) for b in range(2)]
            EB = [pa.enter_context(nc.sbuf_tensor(f"EB{b}", [128, 1024], BF16)) for b in range(3)]
            Esum = pa.enter_context(nc.sbuf_tensor("Esum", [128, 1024], F32))
            R1H = [pa.enter_context(nc.sbuf_tensor(f"R1H{b}", [128, NSLOT * 128], F32)) for b in range(2)]
            Osb = pa.enter_context(nc.sbuf_tensor("Osb", [128, 1024], BF16))
            rr_pad = pa.enter_context(nc.sbuf_tensor("rr_pad", [128, 1024], F32))
            lnr_sb = pa.enter_context(nc.sbuf_tensor("lnr_sb", [1, 1024], F32))
            rcol = pa.enter_context(nc.sbuf_tensor("rcol", [128, 8], F32))
            shift_sb = pa.enter_context(nc.sbuf_tensor("shift_sb", [128, 1], F32))
            T0 = pa.enter_context(nc.psum_tensor("T0", [128, 1024], F32))
            T1 = pa.enter_context(nc.psum_tensor("T1", [128, 1024], F32))
            Ops_ = pa.enter_context(nc.psum_tensor("Ops", [128, 1024], F32))
            NR = pa.enter_context(nc.psum_tensor("NR", [128, 1024], F32))
            Tps = [T0, T1]

            plan.wait("gpsimd", "gp", plan.val("gp"))
            plan.op("gpsimd", lambda e: e.memset(rr_pad[:], 0.0))
            plan.op_inc("gpsimd", "gp", 1, lambda e: e.memset(shift_sb[:], -SHIFT))
            gp_rr = plan.val("gp")
            plan.wait("sync", "st_a", st_phase0)
            plan.op_inc(
                "sync", "ld_s", 16,
                lambda e: e.dma_start(
                    out=mask_sb.rearrange("p (kt c) -> p kt c", kt=NKT), in_=masks_t[:, :, :]
                ),
            )


            plan.wait("sync", "st_a", st_phase0)
            ld_xo = plan.op_inc(
                "sync", "ld_s", 16,
                lambda e: e.dma_start(
                    out=xoin.rearrange("p (t c) -> p t c", t=NSLOT), in_=xno_t[:, :, :]
                ),
            )
            plan.wait("tensor", "ld_s", ld_xo)
            plan.wait("tensor", "gp", gp_rr)
            plan.wait("tensor", "dve", dve_scales)
            qprep_evac = [0, 0]
            for hq in range(NH):
                Tq = Tps[hq % 2]
                if qprep_evac[hq % 2]:
                    plan.wait("tensor", "act", qprep_evac[hq % 2])
                for s_ in range(NSLOT):
                    fn = lambda e, hq=hq, s_=s_, Tq=Tq: e.matmul(
                        Tq[:, s_ * 128 : (s_ + 1) * 128],
                        xoin[:, s_ * DM + hq * HD : s_ * DM + (hq + 1) * HD],
                        ident_b[:, :],
                        start=True, stop=True, skip_group_check=True,
                    )
                    if s_ == NSLOT - 1:
                        vq = plan.op_inc("tensor", "pe", 1, fn)
                    else:
                        plan.op("tensor", fn)
                plan.wait("scalar", "pe", vq)
                qprep_evac[hq % 2] = plan.op_inc(
                    "scalar", "act", 1,
                    lambda e, hq=hq, Tq=Tq: e.activation(
                        QTall[:, hq * TPC : (hq + 1) * TPC], Tq[:, :], AF.Copy
                    ),
                )
            qprep_act = plan.val("act")

            pe_nr_prev = [0]
            h_lds = [0] * NH
            h_ldg = [0] * NH
            h_done = [0] * NH
            h_r1st = [0] * NH

            def emit_loads(h):
                par = h % 2
                if h < 2:
                    plan.wait("sync", "st_a", st_phase0)
                    plan.wait("gpsimd", "st_a", st_phase0)
                else:
                    plan.wait("sync", "dve", h_done[h - 2])
                    plan.wait("gpsimd", "dve", h_done[h - 2])
                plan.wait("sync", "ld_g", plan.val("ld_g"))
                plan.wait("sync", "st_g", plan.val("st_g"))
                plan.wait("sync", "st_a", plan.val("st_a"))
                plan.wait("sync", "ld_s", plan.val("ld_s"))
                plan.wait("sync", "st_s", plan.val("st_s"))
                h_lds[h] = plan.op_inc(
                    "sync", "ld_s", 16,
                    lambda e, h=h, par=par: e.dma_start(
                        out=XT[par][:], in_=xn[:, h * HD : (h + 1) * HD], transpose=True
                    ),
                )
                plan.wait("gpsimd", "ld_s", h_lds[h])
                plan.op_inc(
                    "gpsimd", "ld_g", 16,
                    lambda e, h=h, par=par: e.dma_start(
                        out=VT[par].rearrange("p (kt c) -> p kt c", kt=NKT),
                        in_=xn_kv[:, :, h * HD : (h + 1) * HD],
                    ),
                )
                h_ldg[h] = plan.op_inc(
                    "gpsimd", "ld_g", 16,
                    lambda e, h=h, par=par: e.dma_start(
                        out=XO[par].rearrange("p (s c) -> p s c", s=NSLOT),
                        in_=xo_t[:, :, h * HD : (h + 1) * HD],
                    ),
                )

            emit_loads(0)
            emit_loads(1)

            for h in range(NH):
                par = h % 2
                exp_done = [0] * NKT
                msk_done = [0] * NKT
                esum_done = [0] * NKT
                sgrp = [0] * NKT
                pvgrp = [0] * NKT

                def pe_S(kt, h=h, par=par):
                    act = n_active(kt)
                    na = min(act, 4)
                    nb = act - na
                    T = Tps[kt % 2]
                    if kt < 2:
                        if h > 0:
                            plan.wait("tensor", "dve", h_done[h - 1])
                    else:
                        plan.wait("tensor", "act", exp_done[kt - 2])
                    if kt == 0:
                        plan.wait("tensor", "ld_s", h_lds[h])
                        plan.wait("tensor", "ld_g", h_ldg[h])
                        if h == 0:
                            plan.wait("tensor", "act", qprep_act)
                    if nb:
                        plan.op(
                            "tensor",
                            lambda e, T=T, kt=kt, na=na, par=par, h=h: e.matmul(
                                T[:, 0 : na * 128],
                                XT[par][:, kt * 128 : (kt + 1) * 128],
                                QTall[:, h * TPC : h * TPC + na * 128],
                                start=True, stop=True,
                            ),
                        )
                        sgrp[kt] = plan.op_inc(
                            "tensor", "pe", 1,
                            lambda e, T=T, kt=kt, nb=nb, par=par, h=h: e.matmul(
                                T[:, 512 : 512 + nb * 128],
                                XT[par][:, kt * 128 : (kt + 1) * 128],
                                QTall[:, h * TPC + 512 : h * TPC + 512 + nb * 128],
                                start=True, stop=True,
                            ),
                        )
                    else:
                        sgrp[kt] = plan.op_inc(
                            "tensor", "pe", 1,
                            lambda e, T=T, kt=kt, na=na, par=par, h=h: e.matmul(
                                T[:, 0 : na * 128],
                                XT[par][:, kt * 128 : (kt + 1) * 128],
                                QTall[:, h * TPC : h * TPC + na * 128],
                                start=True, stop=True,
                            ),
                        )

                def pe_PV(kt, h=h, par=par):
                    act = n_active(kt)
                    if kt == 0:
                        plan.wait("tensor", "dve", msk_done[0])
                        plan.op(
                            "tensor",
                            lambda e, par=par: e.matmul(
                                Ops_[:, 0:512],
                                VT[par][:, 0:128],
                                EB[0][:, 0:512],
                                start=True, stop=False,
                                skip_group_check=True,
                            ),
                        )
                        pvgrp[kt] = plan.op_inc(
                            "tensor", "pe", 1,
                            lambda e, par=par: e.matmul(
                                Ops_[:, 512:1024],
                                VT[par][:, 0:128],
                                EB[0][:, 512:1024],
                                start=True, stop=False,
                                skip_group_check=True,
                            ),
                        )
                        return
                    na = min(act, 4)
                    nb = act - na
                    plan.wait("tensor", "dve", msk_done[kt])
                    fn1 = lambda e, kt=kt, na=na, par=par: e.matmul(
                        Ops_[:, 0 : na * 128],
                        VT[par][:, kt * 128 : (kt + 1) * 128],
                        EB[kt % 3][:, 0 : na * 128],
                        start=False, stop=(kt == NKT - 1),
                        skip_group_check=True,
                    )
                    if nb:
                        plan.op("tensor", fn1)
                        pvgrp[kt] = plan.op_inc(
                            "tensor", "pe", 1,
                            lambda e, kt=kt, nb=nb, par=par: e.matmul(
                                Ops_[:, 512 : 512 + nb * 128],
                                VT[par][:, kt * 128 : (kt + 1) * 128],
                                EB[kt % 3][:, 512 : 512 + nb * 128],
                                start=False, stop=(kt == 31),
                                skip_group_check=True,
                            ),
                        )
                    else:
                        pvgrp[kt] = plan.op_inc("tensor", "pe", 1, fn1)

                def act_exp(kt, h=h):
                    act = n_active(kt)
                    na = min(act, 4)
                    nb = act - na
                    T = Tps[kt % 2]
                    plan.wait("scalar", "pe", sgrp[kt])
                    if kt >= 3:
                        plan.wait("scalar", "pe", pvgrp[kt - 3])
                        plan.wait("scalar", "dve", esum_done[kt - 3])
                    exp_done[kt] = plan.op_inc(
                        "scalar", "act", 1,
                        lambda e, T=T, kt=kt, act=act: e.activation(
                            EB[kt % 3][:, 0 : act * 128],
                            T[:, 0 : act * 128],
                            AF.Exp, bias=shift_sb[:, 0:1], scale=SCALE,
                        ),
                    )

                def dve_mask(kt, h=h):
                    act = n_active(kt)
                    plan.wait("vector", "act", exp_done[kt])
                    msk_done[kt] = plan.op_inc(
                        "vector", "dve", 1,
                        lambda e, kt=kt, act=act: e.tensor_tensor(
                            EB[kt % 3][:, (act - 1) * 128 : act * 128],
                            EB[kt % 3][:, (act - 1) * 128 : act * 128],
                            mask_sb[:, kt * 128 : (kt + 1) * 128],
                            ALU.mult,
                        ),
                    )
                    if kt == 0:
                        if h > 0:
                            plan.wait("vector", "pe", pe_nr_prev[0])
                        fn = lambda e: e.tensor_copy(Esum[:, :], EB[0][:, :])
                    else:
                        fn = lambda e, kt=kt, act=act: e.tensor_tensor(
                            Esum[:, 0 : act * 128],
                            Esum[:, 0 : act * 128],
                            EB[kt % 3][:, 0 : act * 128],
                            ALU.add,
                        )
                    esum_done[kt] = plan.op_inc("vector", "dve", 1, fn)

                pe_S(0)
                for kt in range(NKT):
                    act_exp(kt)
                    dve_mask(kt)
                    if kt + 1 < NKT:
                        pe_S(kt + 1)
                    pe_PV(kt)

                plan.wait("tensor", "dve", esum_done[NKT - 1])
                plan.op(
                    "tensor",
                    lambda e: e.matmul(
                        NR[0:1, 0:512], ones_f[:, :], Esum[:, 0:512],
                        start=True, stop=True, skip_group_check=True,
                    ),
                )
                pe_nr = plan.op_inc(
                    "tensor", "pe", 1,
                    lambda e: e.matmul(
                        NR[0:1, 512:1024], ones_f[:, :], Esum[:, 512:1024],
                        start=True, stop=True, skip_group_check=True,
                    ),
                )
                pe_nr_prev[0] = pe_nr
                plan.wait("scalar", "pe", pe_nr)
                if h > 0:
                    plan.wait("scalar", "dve", h_done[h - 1])
                plan.op(
                    "scalar",
                    lambda e: e.activation(Osb[:, :], Ops_[:, :], AF.Copy),
                )
                plan.op("scalar", lambda e: e.activation(lnr_sb[0:1, :], NR[0:1, :], AF.Ln))
                act_rr = plan.op_inc(
                    "scalar", "act", 1,
                    lambda e: e.activation(rr_pad[0:1, :], lnr_sb[0:1, :], AF.Exp, scale=-1.0),
                )
                act_osb = act_rr
                plan.wait("tensor", "act", act_rr)
                for _ in range(8):
                    plan.op(
                        "tensor",
                        lambda e: e.matmul(
                            NR[0:128, 0:128], ident_b[:, :], ident_b[:, :],
                            start=True, stop=True, skip_group_check=True,
                        ),
                    )
                pe_otp = 0
                for s_ in range(NSLOT):
                    fn = lambda e, s_=s_: e.matmul(
                        Ops_[:, s_ * 128 : (s_ + 1) * 128],
                        Osb[:, s_ * 128 : (s_ + 1) * 128],
                        ident_b[:, :],
                        start=True, stop=True, skip_group_check=True,
                    )
                    if s_ == NSLOT - 1:
                        pe_otp = plan.op_inc("tensor", "pe", 1, fn)
                    else:
                        plan.op("tensor", fn)
                pe_pad = 0
                for s_ in range(NSLOT):
                    fn = lambda e, s_=s_: e.matmul(
                        NR[:, s_ * 128 : (s_ + 1) * 128],
                        rr_pad[:, s_ * 128 : (s_ + 1) * 128],
                        ident_f[:, :],
                        start=True, stop=True, skip_group_check=True,
                    )
                    if s_ == NSLOT - 1:
                        pe_pad = plan.op_inc("tensor", "pe", 1, fn)
                    else:
                        plan.op("tensor", fn)
                plan.wait("scalar", "pe", pe_pad)
                act_rcol = plan.op_inc(
                    "scalar", "act", 1,
                    lambda e: e.activation(
                        rcol.rearrange("p (s o) -> p s o", o=1),
                        NR.rearrange("p (s k) -> p s k", k=128)[:, :, 0:1],
                        AF.Copy,
                    ),
                )
                plan.wait("vector", "act", act_rcol)
                plan.wait("vector", "pe", pe_otp)
                if h >= 2:
                    plan.wait("vector", "st_s", h_r1st[h - 2])
                for s_ in range(NSLOT):
                    fn = lambda e, s_=s_, par=par: e.scalar_tensor_tensor(
                        R1H[par][:, s_ * 128 : (s_ + 1) * 128],
                        Ops_[:, s_ * 128 : (s_ + 1) * 128],
                        rcol[:, s_ : s_ + 1],
                        XO[par][:, s_ * 128 : (s_ + 1) * 128],
                        ALU.mult, ALU.add,
                    )
                    if s_ == NSLOT - 1:
                        h_done[h] = plan.op_inc("vector", "dve", 1, fn)
                    else:
                        plan.op("vector", fn)
                plan.wait("sync", "dve", h_done[h])
                plan.wait("sync", "ld_s", plan.val("ld_s"))
                h_r1st[h] = plan.op_inc(
                    "sync", "st_s", 16,
                    lambda e, h=h, par=par: e.dma_start(
                        out=r1_t[:, :, h * HD : (h + 1) * HD],
                        in_=R1H[par].rearrange("p (s c) -> p s c", s=NSLOT),
                    ),
                )
                if debug:
                    plan.op_inc(
                        "sync", "st_s", 16,
                        lambda e, h=h: e.dma_start(
                            out=dbg_qt[h * 128 : (h + 1) * 128, :],
                            in_=QTall[:, h * TPC : (h + 1) * TPC],
                        ),
                    )
                    plan.op_inc(
                        "sync", "st_s", 16,
                        lambda e, h=h: e.dma_start(
                            out=dbg_osb[h * 128 : (h + 1) * 128, :], in_=Osb[:, :]
                        ),
                    )
                    plan.op_inc(
                        "sync", "st_s", 16,
                        lambda e, h=h: e.dma_start(
                            out=dbg_nr[h : h + 1, :], in_=rr_pad[0:1, :]
                        ),
                    )
                    h_r1st[h] = plan.op_inc(
                        "sync", "st_s", 16,
                        lambda e, h=h: e.dma_start(
                            out=dbg_rcol[h * 128 : (h + 1) * 128, :], in_=rcol[:, :]
                        ),
                    )
                if h + 2 < NH:
                    emit_loads(h + 2)

            st_attn = plan.val("st_s")

            if debug:
                plan.wait("gpsimd", "st_s", st_attn)
                dbgr = plan.op_inc(
                    "gpsimd", "st_g", 16,
                    lambda e: e.dma_start(out=dbg_r1[:, :], in_=r1[:, :]),
                )
                plan.wait("sync", "st_g", dbgr)

        hdn = sb("hdn", [128, NIB * TPC], BF16)
        with ExitStack() as pg:
            xn2_fm = pg.enter_context(nc.sbuf_tensor("xn2_fm", [128, NHC * TPC], BF16))
            with ExitStack() as pl:
                r1sb = pl.enter_context(nc.sbuf_tensor("r1sb", [128, 4 * DM], F32))
                x2tm = pl.enter_context(nc.sbuf_tensor("x2tm", [128, NSLOT * DM], BF16))
                sq2 = pl.enter_context(nc.sbuf_tensor("sq2", [128, DM], BF16))
                st2 = pl.enter_context(nc.sbuf_tensor("st2", [128, NSLOT], F32))
                st2b = pl.enter_context(nc.sbuf_tensor("st2b", [128, NSLOT], F32))
                st2c = pl.enter_context(nc.sbuf_tensor("st2c", [128, NSLOT], F32))
                st2d = pl.enter_context(nc.sbuf_tensor("st2d", [128, NSLOT], F32))
                st2e = pl.enter_context(nc.sbuf_tensor("st2e", [128, NSLOT], F32))
                st2f = pl.enter_context(nc.sbuf_tensor("st2f", [128, NSLOT], F32))
                ps_l = pl.enter_context(nc.psum_tensor("ps_l", [128, 1024], F32))
                ps_l2 = pl.enter_context(nc.psum_tensor("ps_l2", [128, 1024], F32))
                ps_lb = [ps_l, ps_l2]

                dve_x2 = 0
                for hf in range(2):
                    ts0 = hf * 4
                    plan.wait("sync", "st_s", st_attn)
                    plan.wait("sync", "ld_s", plan.val("ld_s"))
                    if hf == 1:
                        plan.wait("sync", "dve", dve_x2)
                    ld_r1 = plan.op_inc(
                        "sync", "ld_s", 16,
                        lambda e, ts0=ts0: e.dma_start(
                            out=r1sb.rearrange("p (t c) -> p t c", t=4),
                            in_=r1_t[:, ts0 : ts0 + 4, :],
                        ),
                    )
                    plan.wait("scalar", "ld_s", ld_r1)
                    for t in range(4):
                        plan.op(
                            "scalar",
                            lambda e, t=t, ts0=ts0: e.activation(
                                sq2[:], r1sb[:, t * DM : (t + 1) * DM], AF.Square,
                                accum_out=st2b[:, ts0 + t : ts0 + t + 1],
                            ),
                        )
                    for t in range(4):
                        plan.op(
                            "scalar",
                            lambda e, t=t, ts0=ts0: e.activation(
                                sq2[:], r1sb[:, t * DM : (t + 1) * DM], AF.Copy,
                                accum_out=st2[:, ts0 + t : ts0 + t + 1],
                            ),
                        )
                    cc = slice(ts0, ts0 + 4)
                    plan.op("scalar", lambda e, cc=cc: e.activation(st2[:, cc], st2[:, cc], AF.Copy, scale=-1.0 / DM))
                    a_m2 = plan.op_inc(
                        "scalar", "act", 1,
                        lambda e, cc=cc: e.activation(st2e[:, cc], st2[:, cc], AF.Square),
                    )
                    plan.wait("vector", "act", a_m2)
                    d_m2 = plan.op_inc(
                        "vector", "dve", 1,
                        lambda e, cc=cc: e.tensor_scalar(st2e[:, cc], st2e[:, cc], -1.0, EPS, ALU.mult, ALU.add),
                    )
                    plan.wait("scalar", "dve", d_m2)
                    plan.op("scalar", lambda e, cc=cc: e.activation(st2b[:, cc], st2b[:, cc], AF.Copy, scale=1.0 / DM))
                    act_sq2 = 0
                    for t in range(4):
                        ci = slice(ts0 + t, ts0 + t + 1)
                        plan.op("scalar", lambda e, ci=ci: e.activation(st2f[:, ci], st2b[:, ci], AF.Ln, bias=st2e[:, ci]))
                        fn = lambda e, ci=ci: e.activation(st2c[:, ci], st2f[:, ci], AF.Exp, scale=-0.5)
                        if t == 3:
                            act_sq2 = plan.op_inc("scalar", "act", 1, fn)
                        else:
                            plan.op("scalar", fn)
                    plan.wait("vector", "act", act_sq2)
                    for t in range(4):
                        fn = lambda e, t=t, ts0=ts0: e.tensor_scalar(
                            x2tm[:, (ts0 + t) * DM : (ts0 + t + 1) * DM],
                            r1sb[:, t * DM : (t + 1) * DM],
                            st2[:, ts0 + t : ts0 + t + 1], st2c[:, ts0 + t : ts0 + t + 1],
                            ALU.add, ALU.mult,
                        )
                        if t == 3:
                            dve_x2 = plan.op_inc("vector", "dve", 1, fn)
                        else:
                            plan.op("vector", fn)
                plan.wait("tensor", "dve", dve_x2)
                for _ in range(8):
                    plan.op(
                        "tensor",
                        lambda e: e.matmul(
                            ps_l[:, 0:128], ident_b[:, :], ident_b[:, :],
                            start=True, stop=True, skip_group_check=True,
                        ),
                    )
                act_ln2 = 0
                ev_ln2 = [0] * NHC
                for hc in range(NHC):
                    psl = ps_lb[hc % 2]
                    if hc >= 2:
                        plan.wait("tensor", "act", ev_ln2[hc - 2])
                    for t in range(NSLOT):
                        fn = lambda e, hc=hc, t=t, psl=psl: e.matmul(
                            psl[:, t * 128 : (t + 1) * 128],
                            x2tm[:, t * DM + hc * 128 : t * DM + (hc + 1) * 128],
                            ident_b[:, :],
                            start=True, stop=True, skip_group_check=True,
                        )
                        if t == NSLOT - 1:
                            v = plan.op_inc("tensor", "pe", 1, fn)
                        else:
                            plan.op("tensor", fn)
                    plan.wait("scalar", "pe", v)
                    act_ln2 = plan.op_inc(
                        "scalar", "act", 1,
                        lambda e, hc=hc, psl=psl: e.activation(
                            xn2_fm[:, hc * TPC : (hc + 1) * TPC], psl[:, :], AF.Copy
                        ),
                    )
                    ev_ln2[hc] = act_ln2

            wgt = [pg.enter_context(nc.sbuf_tensor(f"wgt{b}", [128, DM], BF16)) for b in range(2)]
            wut = [pg.enter_context(nc.sbuf_tensor(f"wut{b}", [128, DM], BF16)) for b in range(2)]
            gsil = pg.enter_context(nc.sbuf_tensor("gsil", [128, TPC], BF16))
            gate_ps = pg.enter_context(nc.psum_tensor("gate_ps", [128, 1024], F32))
            up_ps = pg.enter_context(nc.psum_tensor("up_ps", [128, 1024], F32))

            ib_mm = [0] * NIB
            ib_sil = [0] * NIB
            ib_ups = [0] * NIB

            for ib in range(NIB):
                b = ib % 2
                if ib < 2:
                    plan.wait("gpsimd", "act", act_ln2)
                else:
                    plan.wait("gpsimd", "pe", ib_mm[ib - 2])
                plan.op_inc(
                    "gpsimd", "ld_g", 16,
                    lambda e, ib=ib, b=b: e.dma_start(
                        out=wgt[b][:], in_=wg_in[ib * 128 : (ib + 1) * 128, :]
                    ),
                )
                ib_ld = plan.op_inc(
                    "gpsimd", "ld_g", 16,
                    lambda e, ib=ib, b=b: e.dma_start(
                        out=wut[b][:], in_=wu_in[ib * 128 : (ib + 1) * 128, :]
                    ),
                )
                plan.wait("tensor", "ld_g", ib_ld)
                if ib == 0:
                    plan.wait("tensor", "act", act_ln2)
                if ib >= 1:
                    plan.wait("tensor", "act", ib_sil[ib - 1])
                    plan.wait("tensor", "dve", ib_ups[ib - 1])
                for wsel in range(2):
                    wt = wgt[b] if wsel == 0 else wut[b]
                    ps = gate_ps if wsel == 0 else up_ps
                    for hc in range(NHC):
                        for half in range(2):
                            fn = lambda e, wt=wt, ps=ps, hc=hc, half=half: e.matmul(
                                ps[:, half * 512 : (half + 1) * 512],
                                wt[:, hc * 128 : (hc + 1) * 128],
                                xn2_fm[:, hc * TPC + half * 512 : hc * TPC + half * 512 + 512],
                                start=(hc == 0), stop=(hc == NHC - 1),
                                skip_group_check=True,
                            )
                            if hc == NHC - 1 and half == 1:
                                v = plan.op_inc("tensor", "pe", 1, fn)
                            else:
                                plan.op("tensor", fn)
                ib_mm[ib] = v
                plan.wait("scalar", "pe", ib_mm[ib])
                if ib >= 1:
                    plan.wait("scalar", "dve", ib_ups[ib - 1])
                ib_sil[ib] = plan.op_inc(
                    "scalar", "act", 1,
                    lambda e, ib=ib: e.activation(
                        gsil[:, :], gate_ps[:, :], AF.Silu, scale=gate_s_sb[:, ib : ib + 1]
                    ),
                )
                plan.wait("vector", "pe", ib_mm[ib])
                plan.wait("vector", "act", ib_sil[ib])
                ib_ups[ib] = plan.op_inc(
                    "vector", "dve", 1,
                    lambda e, ib=ib: e.scalar_tensor_tensor(
                        hdn[:, ib * TPC : (ib + 1) * TPC],
                        up_ps[:, :],
                        up_s_sb[:, ib : ib + 1],
                        gsil[:, :],
                        ALU.mult, ALU.mult,
                    ),
                )
            gate_end_pe = ib_mm[NIB - 1]
            gate_end_act = ib_sil[NIB - 1]
            gate_end_dve = ib_ups[NIB - 1]

        with ExitStack() as pd:
            wdt = [pd.enter_context(nc.sbuf_tensor(f"wdt{b}", [128, NIB * 128], BF16)) for b in range(2)]
            dsc = pd.enter_context(nc.sbuf_tensor("dsc", [128, TPC], F32))
            r1c = [pd.enter_context(nc.sbuf_tensor(f"r1c{b}", [128, TPC], F32)) for b in range(2)]
            outst = [pd.enter_context(nc.sbuf_tensor(f"outst{b}", [128, TPC], F32)) for b in range(2)]
            down_ps = pd.enter_context(nc.psum_tensor("down_ps", [128, 1024], F32))
            aux_ps = pd.enter_context(nc.psum_tensor("aux_ps", [128, 1024], F32))

            ht_mm = [0] * NHT
            ht_dsc = [0] * NHT
            ht_tm = [0] * NHT
            ht_add = [0] * NHT
            ht_st = [0] * NHT

            for ht in range(NHT):
                b = ht % 2
                if ht < 2:
                    plan.wait("gpsimd", "pe", gate_end_pe)
                    plan.wait("gpsimd", "dve", gate_end_dve)
                    plan.wait("gpsimd", "act", gate_end_act)
                else:
                    plan.wait("gpsimd", "pe", ht_mm[ht - 2])
                    plan.wait("gpsimd", "st_s", ht_st[ht - 2])
                plan.op_inc(
                    "gpsimd", "ld_g", 16,
                    lambda e, ht=ht, b=b: e.dma_start(
                        out=wdt[b][:], in_=wd_in[ht * 128 : (ht + 1) * 128, :]
                    ),
                )
                ht_ld = plan.op_inc(
                    "gpsimd", "ld_g", 16,
                    lambda e, ht=ht, b=b: e.dma_start(
                        out=r1c[b].rearrange("p (s c) -> p s c", s=NSLOT),
                        in_=r1_t[:, :, ht * 128 : (ht + 1) * 128],
                    ),
                )
                plan.wait("tensor", "ld_g", ht_ld)
                if ht == 0:
                    plan.wait("tensor", "act", gate_end_act)
                    plan.wait("tensor", "dve", gate_end_dve)
                if ht >= 1:
                    plan.wait("tensor", "act", ht_dsc[ht - 1])
                    plan.wait("tensor", "dve", ht_add[ht - 1])
                for ib in range(NIB):
                    for half in range(2):
                        fn = lambda e, b=b, ib=ib, half=half: e.matmul(
                            down_ps[:, half * 512 : (half + 1) * 512],
                            wdt[b][:, ib * 128 : (ib + 1) * 128],
                            hdn[:, ib * TPC + half * 512 : ib * TPC + half * 512 + 512],
                            start=(ib == 0), stop=(ib == NIB - 1),
                            skip_group_check=True,
                        )
                        if ib == NIB - 1 and half == 1:
                            v = plan.op_inc("tensor", "pe", 1, fn)
                        else:
                            plan.op("tensor", fn)
                ht_mm[ht] = v
                plan.wait("scalar", "pe", ht_mm[ht])
                if ht >= 1:
                    plan.wait("scalar", "pe", ht_tm[ht - 1])
                ht_dsc[ht] = plan.op_inc(
                    "scalar", "act", 1,
                    lambda e, ht=ht: e.activation(
                        dsc[:, :], down_ps[:, :], AF.Copy, scale=down_s_sb[:, ht : ht + 1]
                    ),
                )
                plan.wait("tensor", "act", ht_dsc[ht])
                for ts in range(NSLOT):
                    fn = lambda e, ts=ts: e.matmul(
                        aux_ps[:, ts * 128 : (ts + 1) * 128],
                        dsc[:, ts * 128 : (ts + 1) * 128],
                        ident_f[:, :],
                        start=True, stop=True, skip_group_check=True,
                    )
                    if ts == NSLOT - 1:
                        ht_tm[ht] = plan.op_inc("tensor", "pe", 1, fn)
                    else:
                        plan.op("tensor", fn)
                plan.wait("vector", "pe", ht_tm[ht])
                plan.wait("vector", "ld_g", ht_ld)
                if ht >= 2:
                    plan.wait("vector", "st_s", ht_st[ht - 2])
                ht_add[ht] = plan.op_inc(
                    "vector", "dve", 1,
                    lambda e, b=b: e.tensor_tensor(
                        outst[b][:, :], aux_ps[:, :], r1c[b][:, :], ALU.add
                    ),
                )
                plan.wait("sync", "dve", ht_add[ht])
                ht_st[ht] = plan.op_inc(
                    "sync", "st_s", 16,
                    lambda e, ht=ht, b=b: e.dma_start(
                        out=out_t[:, :, ht * 128 : (ht + 1) * 128],
                        in_=outst[b].rearrange("p (s c) -> p s c", s=NSLOT),
                    ),
                )

        if dump_xno:
            plan.wait("gpsimd", "st_s", plan.val("st_s"))
            plan.wait("gpsimd", "st_a", plan.val("st_a"))
            v_x2 = plan.op_inc(
                "gpsimd", "st_g", 16,
                lambda e: e.dma_start(out=dbg_xno2[:, :], in_=xn_own[:, :]),
            )
        plan.wait("sync", "st_s", plan.val("st_s"))
        plan.wait("sync", "st_a", plan.val("st_a"))
        plan.wait("sync", "st_g", plan.val("st_g"))

        with nc.Block() as block:
            @block.sync
            def _(e):
                for fn in plan.ops["sync"]:
                    fn(e, sems)

            @block.gpsimd
            def _(e):
                for fn in plan.ops["gpsimd"]:
                    fn(e, sems)

            @block.tensor
            def _(e):
                for fn in plan.ops["tensor"]:
                    fn(e, sems)

            @block.scalar
            def _(e):
                for fn in plan.ops["scalar"]:
                    fn(e, sems)

            @block.vector
            def _(e):
                for fn in plan.ops["vector"]:
                    fn(e, sems)

    return nc


def _host_prep(L0_attn_x, gate_w, up_w, down_w, gate_s, up_s, down_s):
    x = np.ascontiguousarray(np.asarray(L0_attn_x, dtype=np.float32).reshape(S, DM))
    gate_w = np.ascontiguousarray(
        np.asarray(gate_w, dtype=np.float32)
        .reshape(NIB, 128, NHC, 128).transpose(0, 3, 2, 1).reshape(INTER, DM)
    )
    up_w = np.ascontiguousarray(
        np.asarray(up_w, dtype=np.float32)
        .reshape(NIB, 128, NHC, 128).transpose(0, 3, 2, 1).reshape(INTER, DM)
    )
    down_w = np.ascontiguousarray(
        np.asarray(down_w, dtype=np.float32)
        .reshape(NHT, 128, NIB, 128).transpose(0, 3, 2, 1).reshape(DM, INTER)
    )
    gs = np.ascontiguousarray(np.asarray(gate_s, dtype=np.float32).reshape(NIB, 128))
    us = np.ascontiguousarray(np.asarray(up_s, dtype=np.float32).reshape(NIB, 128))
    ds = np.ascontiguousarray(np.asarray(down_s, dtype=np.float32).reshape(NHT, 128))
    idb = np.eye(128, dtype=ml_dtypes.bfloat16)
    idf = np.eye(128, dtype=np.float32)

    tri = np.tril(np.ones((128, 128), dtype=np.float32)).T.astype(ml_dtypes.bfloat16)
    onesm = np.ones((128, 128), dtype=ml_dtypes.bfloat16)
    zerom = np.zeros((128, 128), dtype=ml_dtypes.bfloat16)

    in_maps = []
    for c in range(N_CORES):
        rowsel = [x[slot_qtile(s_, c) * 128 : (slot_qtile(s_, c) + 1) * 128] for s_ in range(NSLOT)]
        x_own = np.ascontiguousarray(np.concatenate(rowsel, axis=0))
        m = np.empty((NKT * 128, 128), dtype=ml_dtypes.bfloat16)
        for kt in range(NKT):
            s_star = (63 - kt) // 8
            d = kt - slot_qtile(s_star, c)
            blk = onesm if d < 0 else (tri if d == 0 else zerom)
            m[kt * 128 : (kt + 1) * 128] = blk
        in_maps.append(
            {
                "x_full": x,
                "x_own": x_own,
                "masks": m,
                "wg": gate_w,
                "wu": up_w,
                "wd": down_w,
                "gs": gs,
                "us": us,
                "ds": ds,
                "idb": idb,
                "idf": idf,
            }
        )
    return in_maps


def kernel(
    L0_attn_x,
    L0_ffn_gate_w,
    L0_ffn_up_w,
    L0_ffn_down_w,
    L0_ffn_gate_s,
    L0_ffn_up_s,
    L0_ffn_down_s,
):
    in_maps = _host_prep(
        L0_attn_x, L0_ffn_gate_w, L0_ffn_up_w, L0_ffn_down_w,
        L0_ffn_gate_s, L0_ffn_up_s, L0_ffn_down_s,
    )
    nc = build_program()
    res = run_bass_kernel_spmd(nc, in_maps, core_ids=list(range(N_CORES)))
    out = np.empty((S, DM), dtype=np.float32)
    for c in range(N_CORES):
        oc = res.results[c]["out"]
        for s_ in range(NSLOT):
            j = slot_qtile(s_, c)
            out[j * 128 : (j + 1) * 128] = oc[s_ * 128 : (s_ + 1) * 128]
    return out.reshape(1, S, DM)


# version 14 (final)
# speedup vs baseline: 1.2139x; 1.2139x over previous
import numpy as np
import ml_dtypes
from contextlib import ExitStack

import concourse.bass as bass
import concourse.mybir as mybir
from concourse.bass_utils import run_bass_kernel_spmd

F32 = mybir.dt.float32
BF16 = mybir.dt.bfloat16
AF = mybir.ActivationFunctionType
ALU = mybir.AluOpType
AXX = mybir.AxisListType.X

S = 8192
DM = 2048
NH = 16
HD = 128
TPC = 1024
NSLOT = 8
NKT = 64
INTER = 5632
NIB = INTER // 128
NHT = DM // 128
NHC = DM // 128
EPS = 1e-5
SCALE = 1.0 / float(np.sqrt(HD))
SHIFT = 12.0
N_CORES = 8


def slot_qtile(s, c):
    return 8 * (7 - s) + c


def slot_count(s):
    return 64 - 8 * s


def n_active(kt):
    return 8 - kt // 8


class Plan:
    ENGINES = ("sync", "gpsimd", "tensor", "scalar", "vector")

    def __init__(self):
        self.ops = {e: [] for e in self.ENGINES}
        self.counts = {}

    def inc(self, name, k=1):
        self.counts[name] = self.counts.get(name, 0) + k
        return self.counts[name]

    def val(self, name):
        return self.counts.get(name, 0)

    def emit(self, engine, fn):
        self.ops[engine].append(fn)

    def wait(self, engine, name, value):
        if value <= 0:
            return
        self.emit(engine, lambda e, sems, name=name, value=value: e.wait_ge(sems[name], value))

    def op_inc(self, engine, name, k, fn):
        v = self.inc(name, k)
        self.emit(
            engine,
            lambda e, sems, fn=fn, name=name, k=k: fn(e).then_inc(sems[name], k),
        )
        return v

    def op(self, engine, fn):
        self.emit(engine, lambda e, sems, fn=fn: fn(e))


def build_program(debug=False, dump_xno=False):
    nc = bass.Bass("TRN2", target_bir_lowering=False)

    x_full = nc.dram_tensor("x_full", [S, DM], F32, kind="ExternalInput").ap()
    x_own = nc.dram_tensor("x_own", [TPC, DM], F32, kind="ExternalInput").ap()
    masks_in = nc.dram_tensor("masks", [NKT * 128, 128], BF16, kind="ExternalInput").ap()
    wg_in = nc.dram_tensor("wg", [INTER, DM], F32, kind="ExternalInput").ap()
    wu_in = nc.dram_tensor("wu", [INTER, DM], F32, kind="ExternalInput").ap()
    wd_in = nc.dram_tensor("wd", [DM, INTER], F32, kind="ExternalInput").ap()
    gs_in = nc.dram_tensor("gs", [NIB, 128], F32, kind="ExternalInput").ap()
    us_in = nc.dram_tensor("us", [NIB, 128], F32, kind="ExternalInput").ap()
    ds_in = nc.dram_tensor("ds", [NHT, 128], F32, kind="ExternalInput").ap()
    idb_in = nc.dram_tensor("idb", [128, 128], BF16, kind="ExternalInput").ap()
    idf_in = nc.dram_tensor("idf", [128, 128], F32, kind="ExternalInput").ap()
    out = nc.dram_tensor("out", [TPC, DM], F32, kind="ExternalOutput").ap()

    xn = nc.dram_tensor("xn", [S, DM], BF16, kind="Internal").ap()
    _g1 = nc.dram_tensor("_g1", [TPC, DM], BF16, kind="Internal").ap()
    xn_own = nc.dram_tensor("xn_own", [TPC, DM], BF16, kind="Internal").ap()
    _g2 = nc.dram_tensor("_g2", [TPC, DM], BF16, kind="Internal").ap()
    r1 = nc.dram_tensor("r1", [TPC, DM], F32, kind="Internal").ap()
    if dump_xno:
        dbg_xno2 = nc.dram_tensor("dbg_xno2", [TPC, DM], F32, kind="ExternalOutput").ap()
    if debug:
        dbg_xn = nc.dram_tensor("dbg_xn", [S, DM], F32, kind="ExternalOutput").ap()
        dbg_r1 = nc.dram_tensor("dbg_r1", [TPC, DM], F32, kind="ExternalOutput").ap()
        dbg_osb = nc.dram_tensor("dbg_osb", [NH * 128, 1024], BF16, kind="ExternalOutput").ap()
        dbg_nr = nc.dram_tensor("dbg_nr", [NH, 1024], F32, kind="ExternalOutput").ap()
        dbg_rcol = nc.dram_tensor("dbg_rcol", [NH * 128, 8], F32, kind="ExternalOutput").ap()
        dbg_xno = nc.dram_tensor("dbg_xno", [TPC, DM], F32, kind="ExternalOutput").ap()
        dbg_qt = nc.dram_tensor("dbg_qt", [NH * 128, TPC], BF16, kind="ExternalOutput").ap()

    xf_t = x_full.rearrange("(t p) c -> p t c", p=128)
    xo_t = x_own.rearrange("(t p) c -> p t c", p=128)
    xn_t = xn.rearrange("(t p) c -> p t c", p=128)
    xno_t = xn_own.rearrange("(t p) c -> p t c", p=128)
    xn_kv = xn.rearrange("(kt r) c -> r kt c", r=128)
    r1_t = r1.rearrange("(t p) c -> p t c", p=128)
    out_t = out.rearrange("(t p) c -> p t c", p=128)
    masks_t = masks_in.rearrange("(kt r) c -> r kt c", r=128)

    plan = Plan()
    stk = ExitStack()

    def sb(name, shape, dt):
        return stk.enter_context(nc.sbuf_tensor(name, shape, dt))

    sems = {}
    for s_ in ("ld_s", "ld_g", "st_s", "st_g", "st_a", "pe", "act", "dve", "gp"):
        sems[s_] = stk.enter_context(nc.semaphore(s_))

    ident_b = sb("ident_b", [128, 128], BF16)
    ident_f = sb("ident_f", [128, 128], F32)
    ones_b = sb("ones_b", [128, 1], BF16)
    ones_f = sb("ones_f", [128, 1], F32)
    eps_sb = sb("eps_sb", [128, 1], F32)
    gate_s_sb = sb("gate_s_sb", [128, NIB], F32)
    up_s_sb = sb("up_s_sb", [128, NIB], F32)
    down_s_sb = sb("down_s_sb", [128, NHT], F32)

    with stk:
        plan.op_inc("sync", "ld_s", 16, lambda e: e.dma_start(out=ident_b[:], in_=idb_in[:, :]))
        plan.op_inc("sync", "ld_s", 16, lambda e: e.dma_start(out=ident_f[:], in_=idf_in[:, :]))
        gs_nat = sb("gs_nat", [NIB, 128], F32)
        us_nat = sb("us_nat", [NIB, 128], F32)
        ds_nat = sb("ds_nat", [NHT, 128], F32)
        plan.op_inc("gpsimd", "ld_g", 16, lambda e: e.dma_start(out=gs_nat[:], in_=gs_in[:, :]))
        plan.op_inc("gpsimd", "ld_g", 16, lambda e: e.dma_start(out=us_nat[:], in_=us_in[:, :]))
        plan.op_inc("gpsimd", "ld_g", 16, lambda e: e.dma_start(out=ds_nat[:], in_=ds_in[:, :]))
        plan.op("gpsimd", lambda e: e.memset(ones_b[:], 1.0))
        plan.op("gpsimd", lambda e: e.memset(ones_f[:], 1.0))
        plan.op_inc("gpsimd", "gp", 1, lambda e: e.memset(eps_sb[:], EPS))
        ld_s_consts = plan.val("ld_s")
        ld_g_consts = plan.val("ld_g")
        gp_eps = plan.val("gp")

        with ExitStack() as pc:
            ps_sc = pc.enter_context(nc.psum_tensor("ps_sc", [128, 64], F32))
            plan.wait("tensor", "ld_s", ld_s_consts)
            plan.wait("tensor", "ld_g", ld_g_consts)
            for nat, dst, n in (
                (gs_nat, gate_s_sb, NIB),
                (us_nat, up_s_sb, NIB),
                (ds_nat, down_s_sb, NHT),
            ):
                pv = plan.val("dve")
                plan.wait("tensor", "dve", pv)
                plan.op_inc(
                    "tensor", "pe", 1,
                    lambda e, nat=nat, n=n: e.matmul(
                        ps_sc[:, 0:n], nat[:, :], ident_f[0:n, 0:n], start=True, stop=True
                    ),
                )
                plan.wait("vector", "pe", plan.val("pe"))
                plan.op_inc(
                    "vector", "dve", 1,
                    lambda e, dst=dst, n=n: e.tensor_copy(dst[:, 0:n], ps_sc[:, 0:n]),
                )
            dve_scales = plan.val("dve")

        NG0 = 36
        with ExitStack() as p0:
            NB0 = 3
            xg = [p0.enter_context(nc.sbuf_tensor(f"xg{b}", [128, 2 * DM], F32)) for b in range(NB0)]
            xn_st = [p0.enter_context(nc.sbuf_tensor(f"xnst{b}", [128, 2 * DM], BF16)) for b in range(NB0)]
            sq_scr = p0.enter_context(nc.sbuf_tensor("sq_scr", [128, DM], BF16))
            sums = p0.enter_context(nc.sbuf_tensor("sums", [128, 2 * NG0], F32))
            sumsq = p0.enter_context(nc.sbuf_tensor("sumsq", [128, 2 * NG0], F32))
            nmean = p0.enter_context(nc.sbuf_tensor("nmean", [128, 2 * NG0], F32))
            msq = p0.enter_context(nc.sbuf_tensor("msq", [128, 2 * NG0], F32))
            nmq = p0.enter_context(nc.sbuf_tensor("nmq", [128, 2 * NG0], F32))
            varp = p0.enter_context(nc.sbuf_tensor("varp", [128, 2 * NG0], F32))
            lnv = p0.enter_context(nc.sbuf_tensor("lnv", [128, 2 * NG0], F32))
            rstd = p0.enter_context(nc.sbuf_tensor("rstd", [128, 2 * NG0], F32))
            nmr = p0.enter_context(nc.sbuf_tensor("nmr", [128, 2 * NG0], F32))

            g_ld = [0] * NG0
            g_stats = [0] * NG0
            g_nrm = [0] * NG0
            g_st = [0] * NG0

            for g in range(NG0):
                b = g % NB0
                own = g >= 32
                t0 = 2 * g if not own else 2 * (g - 32)
                src = xf_t if not own else xo_t
                ldq = "sync" if g % 2 == 0 else "gpsimd"
                ldsem = "ld_s" if g % 2 == 0 else "ld_g"
                if g >= NB0:
                    plan.wait(ldq, "act", g_stats[g - NB0])
                    plan.wait(ldq, "dve", g_nrm[g - NB0])
                g_ld[g] = (ldsem, plan.op_inc(
                    ldq, ldsem, 16,
                    lambda e, b=b, src=src, t0=t0: e.dma_start(
                        out=xg[b].rearrange("p (t c) -> p t c", t=2),
                        in_=src[:, t0 : t0 + 2, :],
                    ),
                ))
                plan.wait("scalar", g_ld[g][0], g_ld[g][1])
                if g == 0:
                    plan.wait("scalar", "gp", gp_eps)
                for i in range(2):
                    plan.op(
                        "scalar",
                        lambda e, b=b, i=i, g=g: e.activation(
                            sq_scr[:], xg[b][:, i * DM : (i + 1) * DM], AF.Square,
                            accum_out=sumsq[:, 2 * g + i : 2 * g + i + 1],
                        ),
                    )
                for i in range(2):
                    plan.op(
                        "scalar",
                        lambda e, b=b, i=i, g=g: e.activation(
                            sq_scr[:], xg[b][:, i * DM : (i + 1) * DM], AF.Copy,
                            accum_out=sums[:, 2 * g + i : 2 * g + i + 1],
                        ),
                    )
                cc = slice(2 * g, 2 * g + 2)
                plan.op("scalar", lambda e, cc=cc: e.activation(nmean[:, cc], sums[:, cc], AF.Copy, scale=-1.0 / DM))
                a_msq = plan.op_inc(
                    "scalar", "act", 1,
                    lambda e, cc=cc: e.activation(msq[:, cc], nmean[:, cc], AF.Square),
                )
                plan.wait("vector", "act", a_msq)
                d_nmq = plan.op_inc(
                    "vector", "dve", 1,
                    lambda e, cc=cc: e.tensor_scalar(nmq[:, cc], msq[:, cc], -1.0, EPS, ALU.mult, ALU.add),
                )
                plan.wait("scalar", "dve", d_nmq)
                plan.op("scalar", lambda e, cc=cc: e.activation(varp[:, cc], sumsq[:, cc], AF.Copy, scale=1.0 / DM))
                for i in range(2):
                    ci = slice(2 * g + i, 2 * g + i + 1)
                    plan.op("scalar", lambda e, ci=ci: e.activation(lnv[:, ci], varp[:, ci], AF.Ln, bias=nmq[:, ci]))
                    fn = lambda e, ci=ci: e.activation(rstd[:, ci], lnv[:, ci], AF.Exp, scale=-0.5)
                    if i == 1:
                        g_stats[g] = plan.op_inc("scalar", "act", 1, fn)
                    else:
                        plan.op("scalar", fn)
                plan.wait("vector", "act", g_stats[g])
                if g >= NB0:
                    plan.wait("vector", "st_a", g_st[g - NB0])
                plan.op(
                    "vector",
                    lambda e, b=b, g=g: e.tensor_scalar(
                        xn_st[b][:, 0:DM], xg[b][:, 0:DM],
                        nmean[:, 2 * g : 2 * g + 1], rstd[:, 2 * g : 2 * g + 1],
                        ALU.add, ALU.mult,
                    ),
                )
                g_nrm[g] = plan.op_inc(
                    "vector", "dve", 1,
                    lambda e, b=b, g=g: e.tensor_scalar(
                        xn_st[b][:, DM : 2 * DM], xg[b][:, DM : 2 * DM],
                        nmean[:, 2 * g + 1 : 2 * g + 2], rstd[:, 2 * g + 1 : 2 * g + 2],
                        ALU.add, ALU.mult,
                    ),
                )
                def emit_store(gq):
                    bq = gq % NB0
                    ownq = gq >= 32
                    tq = 2 * gq if not ownq else 2 * (gq - 32)
                    dstt2 = xn_t if not ownq else xno_t
                    plan.wait("scalar", "dve", g_nrm[gq])
                    g_st[gq] = plan.op_inc(
                        "scalar", "st_a", 16,
                        lambda e, bq=bq, dstt2=dstt2, tq=tq: e.dma_start(
                            out=dstt2[:, tq : tq + 2, :],
                            in_=xn_st[bq].rearrange("p (t c) -> p t c", t=2),
                        ),
                    )
                if g >= 1:
                    emit_store(g - 1)
                if g == NG0 - 1:
                    emit_store(g)
            st_phase0 = plan.val("st_a")

            if debug:
                plan.wait("gpsimd", "st_a", st_phase0)
                plan.op_inc(
                    "gpsimd", "st_g", 16,
                    lambda e: e.dma_start(out=dbg_xn[:, :], in_=xn[:, :]),
                )
                dbg0 = plan.op_inc(
                    "gpsimd", "st_g", 16,
                    lambda e: e.dma_start(out=dbg_xno[:, :], in_=xn_own[:, :]),
                )
                plan.wait("sync", "st_g", dbg0)

        with ExitStack() as pa:
            mask_sb = pa.enter_context(nc.sbuf_tensor("mask_sb", [128, NKT * 128], BF16))
            XT = [pa.enter_context(nc.sbuf_tensor(f"XT{b}", [128, S], BF16)) for b in range(2)]
            QTall = pa.enter_context(nc.sbuf_tensor("QTall", [128, NH * TPC], BF16))
            xoin = pa.enter_context(nc.sbuf_tensor("xoin", [128, NSLOT * DM], BF16))
            VT = [pa.enter_context(nc.sbuf_tensor(f"VT{b}", [128, NKT * 128], BF16)) for b in range(2)]
            XO = [pa.enter_context(nc.sbuf_tensor(f"XO{b}", [128, NSLOT * 128], F32)) for b in range(2)]
            EB = [pa.enter_context(nc.sbuf_tensor(f"EB{b}", [128, 1024], BF16)) for b in range(3)]
            Esum = pa.enter_context(nc.sbuf_tensor("Esum", [128, 1024], F32))
            R1H = [pa.enter_context(nc.sbuf_tensor(f"R1H{b}", [128, NSLOT * 128], F32)) for b in range(2)]
            Osb = pa.enter_context(nc.sbuf_tensor("Osb", [128, 1024], BF16))
            rr_pad = pa.enter_context(nc.sbuf_tensor("rr_pad", [128, 1024], F32))
            lnr_sb = pa.enter_context(nc.sbuf_tensor("lnr_sb", [1, 1024], F32))
            rcol = pa.enter_context(nc.sbuf_tensor("rcol", [128, 8], F32))
            shift_sb = pa.enter_context(nc.sbuf_tensor("shift_sb", [128, 1], F32))
            T0 = pa.enter_context(nc.psum_tensor("T0", [128, 1024], F32))
            T1 = pa.enter_context(nc.psum_tensor("T1", [128, 1024], F32))
            Ops_ = pa.enter_context(nc.psum_tensor("Ops", [128, 1024], F32))
            NR = pa.enter_context(nc.psum_tensor("NR", [128, 1024], F32))
            Tps = [T0, T1]

            plan.wait("gpsimd", "gp", plan.val("gp"))
            plan.op("gpsimd", lambda e: e.memset(rr_pad[:], 0.0))
            plan.op_inc("gpsimd", "gp", 1, lambda e: e.memset(shift_sb[:], -SHIFT))
            gp_rr = plan.val("gp")
            plan.wait("sync", "st_a", st_phase0)
            plan.op_inc(
                "sync", "ld_s", 16,
                lambda e: e.dma_start(
                    out=mask_sb.rearrange("p (kt c) -> p kt c", kt=NKT), in_=masks_t[:, :, :]
                ),
            )


            plan.wait("sync", "st_a", st_phase0)
            ld_xo = plan.op_inc(
                "sync", "ld_s", 16,
                lambda e: e.dma_start(
                    out=xoin.rearrange("p (t c) -> p t c", t=NSLOT), in_=xno_t[:, :, :]
                ),
            )
            plan.wait("tensor", "ld_s", ld_xo)
            plan.wait("tensor", "gp", gp_rr)
            plan.wait("tensor", "dve", dve_scales)
            qprep_evac = [0, 0]
            for hq in range(NH):
                Tq = Tps[hq % 2]
                if qprep_evac[hq % 2]:
                    plan.wait("tensor", "act", qprep_evac[hq % 2])
                for s_ in range(NSLOT):
                    fn = lambda e, hq=hq, s_=s_, Tq=Tq: e.matmul(
                        Tq[:, s_ * 128 : (s_ + 1) * 128],
                        xoin[:, s_ * DM + hq * HD : s_ * DM + (hq + 1) * HD],
                        ident_b[:, :],
                        start=True, stop=True, skip_group_check=True,
                    )
                    if s_ == NSLOT - 1:
                        vq = plan.op_inc("tensor", "pe", 1, fn)
                    else:
                        plan.op("tensor", fn)
                plan.wait("scalar", "pe", vq)
                qprep_evac[hq % 2] = plan.op_inc(
                    "scalar", "act", 1,
                    lambda e, hq=hq, Tq=Tq: e.activation(
                        QTall[:, hq * TPC : (hq + 1) * TPC], Tq[:, :], AF.Copy
                    ),
                )
            qprep_act = plan.val("act")

            pe_nr_prev = [0]
            h_lds = [0] * NH
            h_ldg = [0] * NH
            h_done = [0] * NH
            h_r1st = [0] * NH

            def emit_loads(h):
                par = h % 2
                if h < 2:
                    plan.wait("sync", "st_a", st_phase0)
                    plan.wait("gpsimd", "st_a", st_phase0)
                else:
                    plan.wait("sync", "dve", h_done[h - 2])
                    plan.wait("gpsimd", "dve", h_done[h - 2])
                plan.wait("sync", "ld_g", plan.val("ld_g"))
                plan.wait("sync", "st_g", plan.val("st_g"))
                plan.wait("sync", "st_a", plan.val("st_a"))
                plan.wait("sync", "ld_s", plan.val("ld_s"))
                plan.wait("sync", "st_s", plan.val("st_s"))
                h_lds[h] = plan.op_inc(
                    "sync", "ld_s", 16,
                    lambda e, h=h, par=par: e.dma_start(
                        out=XT[par][:], in_=xn[:, h * HD : (h + 1) * HD], transpose=True
                    ),
                )
                plan.wait("gpsimd", "ld_s", h_lds[h])
                plan.op_inc(
                    "gpsimd", "ld_g", 16,
                    lambda e, h=h, par=par: e.dma_start(
                        out=VT[par].rearrange("p (kt c) -> p kt c", kt=NKT),
                        in_=xn_kv[:, :, h * HD : (h + 1) * HD],
                    ),
                )
                h_ldg[h] = plan.op_inc(
                    "gpsimd", "ld_g", 16,
                    lambda e, h=h, par=par: e.dma_start(
                        out=XO[par].rearrange("p (s c) -> p s c", s=NSLOT),
                        in_=xo_t[:, :, h * HD : (h + 1) * HD],
                    ),
                )

            emit_loads(0)
            emit_loads(1)

            for h in range(NH):
                par = h % 2
                exp_done = [0] * NKT
                msk_done = [0] * NKT
                esum_done = [0] * NKT
                sgrp = [0] * NKT
                pvgrp = [0] * NKT

                def pe_S(kt, h=h, par=par):
                    act = n_active(kt)
                    na = min(act, 4)
                    nb = act - na
                    T = Tps[kt % 2]
                    if kt < 2:
                        if h > 0:
                            plan.wait("tensor", "dve", h_done[h - 1])
                    else:
                        plan.wait("tensor", "act", exp_done[kt - 2])
                    if kt == 0:
                        plan.wait("tensor", "ld_s", h_lds[h])
                        plan.wait("tensor", "ld_g", h_ldg[h])
                        if h == 0:
                            plan.wait("tensor", "act", qprep_act)
                    if nb:
                        plan.op(
                            "tensor",
                            lambda e, T=T, kt=kt, na=na, par=par, h=h: e.matmul(
                                T[:, 0 : na * 128],
                                XT[par][:, kt * 128 : (kt + 1) * 128],
                                QTall[:, h * TPC : h * TPC + na * 128],
                                start=True, stop=True,
                            ),
                        )
                        sgrp[kt] = plan.op_inc(
                            "tensor", "pe", 1,
                            lambda e, T=T, kt=kt, nb=nb, par=par, h=h: e.matmul(
                                T[:, 512 : 512 + nb * 128],
                                XT[par][:, kt * 128 : (kt + 1) * 128],
                                QTall[:, h * TPC + 512 : h * TPC + 512 + nb * 128],
                                start=True, stop=True,
                            ),
                        )
                    else:
                        sgrp[kt] = plan.op_inc(
                            "tensor", "pe", 1,
                            lambda e, T=T, kt=kt, na=na, par=par, h=h: e.matmul(
                                T[:, 0 : na * 128],
                                XT[par][:, kt * 128 : (kt + 1) * 128],
                                QTall[:, h * TPC : h * TPC + na * 128],
                                start=True, stop=True,
                            ),
                        )

                def pe_PV(kt, h=h, par=par):
                    act = n_active(kt)
                    plan.wait("tensor", "act", exp_done[kt])
                    if kt == 0:
                        plan.wait("tensor", "dve", msk_done[0])
                        plan.op(
                            "tensor",
                            lambda e, par=par: e.matmul(
                                Ops_[:, 0:512],
                                VT[par][:, 0:128],
                                EB[0][:, 0:512],
                                start=True, stop=False,
                                skip_group_check=True,
                            ),
                        )
                        pvgrp[kt] = plan.op_inc(
                            "tensor", "pe", 1,
                            lambda e, par=par: e.matmul(
                                Ops_[:, 512:1024],
                                VT[par][:, 0:128],
                                EB[0][:, 512:1024],
                                start=True, stop=False,
                                skip_group_check=True,
                            ),
                        )
                        return
                    na = min(act, 4)
                    nb = act - na
                    m0 = (act - 1) * 128
                    if act > 1:
                        ua = min(act - 1, 4)
                        ub = (act - 1) - ua
                        plan.op(
                            "tensor",
                            lambda e, kt=kt, ua=ua, par=par: e.matmul(
                                Ops_[:, 0 : ua * 128],
                                VT[par][:, kt * 128 : (kt + 1) * 128],
                                EB[kt % 3][:, 0 : ua * 128],
                                start=False, stop=(kt == NKT - 1),
                                skip_group_check=True,
                            ),
                        )
                        if ub:
                            plan.op(
                                "tensor",
                                lambda e, kt=kt, ub=ub, par=par: e.matmul(
                                    Ops_[:, 512 : 512 + ub * 128],
                                    VT[par][:, kt * 128 : (kt + 1) * 128],
                                    EB[kt % 3][:, 512 : 512 + ub * 128],
                                    start=False, stop=False,
                                    skip_group_check=True,
                                ),
                            )
                    plan.wait("tensor", "dve", msk_done[kt])
                    pvgrp[kt] = plan.op_inc(
                        "tensor", "pe", 1,
                        lambda e, kt=kt, m0=m0, par=par: e.matmul(
                            Ops_[:, m0 : m0 + 128],
                            VT[par][:, kt * 128 : (kt + 1) * 128],
                            EB[kt % 3][:, m0 : m0 + 128],
                            start=False, stop=(kt == 31 if m0 >= 512 else kt == NKT - 1),
                            skip_group_check=True,
                        ),
                    )

                def act_exp(kt, h=h):
                    act = n_active(kt)
                    na = min(act, 4)
                    nb = act - na
                    T = Tps[kt % 2]
                    plan.wait("scalar", "pe", sgrp[kt])
                    if kt >= 3:
                        plan.wait("scalar", "pe", pvgrp[kt - 3])
                        plan.wait("scalar", "dve", esum_done[kt - 3])
                    exp_done[kt] = plan.op_inc(
                        "scalar", "act", 1,
                        lambda e, T=T, kt=kt, act=act: e.activation(
                            EB[kt % 3][:, 0 : act * 128],
                            T[:, 0 : act * 128],
                            AF.Exp, bias=shift_sb[:, 0:1], scale=SCALE,
                        ),
                    )

                def dve_mask(kt, h=h):
                    act = n_active(kt)
                    plan.wait("vector", "act", exp_done[kt])
                    msk_done[kt] = plan.op_inc(
                        "vector", "dve", 1,
                        lambda e, kt=kt, act=act: e.tensor_tensor(
                            EB[kt % 3][:, (act - 1) * 128 : act * 128],
                            EB[kt % 3][:, (act - 1) * 128 : act * 128],
                            mask_sb[:, kt * 128 : (kt + 1) * 128],
                            ALU.mult,
                        ),
                    )
                    if kt == 0:
                        if h > 0:
                            plan.wait("vector", "pe", pe_nr_prev[0])
                        fn = lambda e: e.tensor_copy(Esum[:, :], EB[0][:, :])
                    else:
                        fn = lambda e, kt=kt, act=act: e.tensor_tensor(
                            Esum[:, 0 : act * 128],
                            Esum[:, 0 : act * 128],
                            EB[kt % 3][:, 0 : act * 128],
                            ALU.add,
                        )
                    esum_done[kt] = plan.op_inc("vector", "dve", 1, fn)

                pe_S(0)
                for kt in range(NKT):
                    act_exp(kt)
                    dve_mask(kt)
                    if kt + 1 < NKT:
                        pe_S(kt + 1)
                    pe_PV(kt)

                plan.wait("tensor", "dve", esum_done[NKT - 1])
                plan.op(
                    "tensor",
                    lambda e: e.matmul(
                        NR[0:1, 0:512], ones_f[:, :], Esum[:, 0:512],
                        start=True, stop=True, skip_group_check=True,
                    ),
                )
                pe_nr = plan.op_inc(
                    "tensor", "pe", 1,
                    lambda e: e.matmul(
                        NR[0:1, 512:1024], ones_f[:, :], Esum[:, 512:1024],
                        start=True, stop=True, skip_group_check=True,
                    ),
                )
                pe_nr_prev[0] = pe_nr
                plan.wait("scalar", "pe", pe_nr)
                if h > 0:
                    plan.wait("scalar", "dve", h_done[h - 1])
                plan.op(
                    "scalar",
                    lambda e: e.activation(Osb[:, :], Ops_[:, :], AF.Copy),
                )
                plan.op("scalar", lambda e: e.activation(lnr_sb[0:1, :], NR[0:1, :], AF.Ln))
                act_rr = plan.op_inc(
                    "scalar", "act", 1,
                    lambda e: e.activation(rr_pad[0:1, :], lnr_sb[0:1, :], AF.Exp, scale=-1.0),
                )
                act_osb = act_rr
                plan.wait("tensor", "act", act_rr)
                for _ in range(8):
                    plan.op(
                        "tensor",
                        lambda e: e.matmul(
                            NR[0:128, 0:128], ident_b[:, :], ident_b[:, :],
                            start=True, stop=True, skip_group_check=True,
                        ),
                    )
                pe_otp = 0
                for s_ in range(NSLOT):
                    fn = lambda e, s_=s_: e.matmul(
                        Ops_[:, s_ * 128 : (s_ + 1) * 128],
                        Osb[:, s_ * 128 : (s_ + 1) * 128],
                        ident_b[:, :],
                        start=True, stop=True, skip_group_check=True,
                    )
                    if s_ == NSLOT - 1:
                        pe_otp = plan.op_inc("tensor", "pe", 1, fn)
                    else:
                        plan.op("tensor", fn)
                pe_pad = 0
                for s_ in range(NSLOT):
                    fn = lambda e, s_=s_: e.matmul(
                        NR[:, s_ * 128 : (s_ + 1) * 128],
                        rr_pad[:, s_ * 128 : (s_ + 1) * 128],
                        ident_f[:, :],
                        start=True, stop=True, skip_group_check=True,
                    )
                    if s_ == NSLOT - 1:
                        pe_pad = plan.op_inc("tensor", "pe", 1, fn)
                    else:
                        plan.op("tensor", fn)
                plan.wait("scalar", "pe", pe_pad)
                act_rcol = plan.op_inc(
                    "scalar", "act", 1,
                    lambda e: e.activation(
                        rcol.rearrange("p (s o) -> p s o", o=1),
                        NR.rearrange("p (s k) -> p s k", k=128)[:, :, 0:1],
                        AF.Copy,
                    ),
                )
                plan.wait("vector", "act", act_rcol)
                plan.wait("vector", "pe", pe_otp)
                if h >= 2:
                    plan.wait("vector", "st_s", h_r1st[h - 2])
                for s_ in range(NSLOT):
                    fn = lambda e, s_=s_, par=par: e.scalar_tensor_tensor(
                        R1H[par][:, s_ * 128 : (s_ + 1) * 128],
                        Ops_[:, s_ * 128 : (s_ + 1) * 128],
                        rcol[:, s_ : s_ + 1],
                        XO[par][:, s_ * 128 : (s_ + 1) * 128],
                        ALU.mult, ALU.add,
                    )
                    if s_ == NSLOT - 1:
                        h_done[h] = plan.op_inc("vector", "dve", 1, fn)
                    else:
                        plan.op("vector", fn)
                plan.wait("sync", "dve", h_done[h])
                plan.wait("sync", "ld_s", plan.val("ld_s"))
                h_r1st[h] = plan.op_inc(
                    "sync", "st_s", 16,
                    lambda e, h=h, par=par: e.dma_start(
                        out=r1_t[:, :, h * HD : (h + 1) * HD],
                        in_=R1H[par].rearrange("p (s c) -> p s c", s=NSLOT),
                    ),
                )
                if debug:
                    plan.op_inc(
                        "sync", "st_s", 16,
                        lambda e, h=h: e.dma_start(
                            out=dbg_qt[h * 128 : (h + 1) * 128, :],
                            in_=QTall[:, h * TPC : (h + 1) * TPC],
                        ),
                    )
                    plan.op_inc(
                        "sync", "st_s", 16,
                        lambda e, h=h: e.dma_start(
                            out=dbg_osb[h * 128 : (h + 1) * 128, :], in_=Osb[:, :]
                        ),
                    )
                    plan.op_inc(
                        "sync", "st_s", 16,
                        lambda e, h=h: e.dma_start(
                            out=dbg_nr[h : h + 1, :], in_=rr_pad[0:1, :]
                        ),
                    )
                    h_r1st[h] = plan.op_inc(
                        "sync", "st_s", 16,
                        lambda e, h=h: e.dma_start(
                            out=dbg_rcol[h * 128 : (h + 1) * 128, :], in_=rcol[:, :]
                        ),
                    )
                if h + 2 < NH:
                    emit_loads(h + 2)

            st_attn = plan.val("st_s")

            if debug:
                plan.wait("gpsimd", "st_s", st_attn)
                dbgr = plan.op_inc(
                    "gpsimd", "st_g", 16,
                    lambda e: e.dma_start(out=dbg_r1[:, :], in_=r1[:, :]),
                )
                plan.wait("sync", "st_g", dbgr)

        hdn = sb("hdn", [128, NIB * TPC], BF16)
        with ExitStack() as pg:
            xn2_fm = pg.enter_context(nc.sbuf_tensor("xn2_fm", [128, NHC * TPC], BF16))
            with ExitStack() as pl:
                r1sb = pl.enter_context(nc.sbuf_tensor("r1sb", [128, 4 * DM], F32))
                x2tm = pl.enter_context(nc.sbuf_tensor("x2tm", [128, NSLOT * DM], BF16))
                sq2 = pl.enter_context(nc.sbuf_tensor("sq2", [128, DM], BF16))
                st2 = pl.enter_context(nc.sbuf_tensor("st2", [128, NSLOT], F32))
                st2b = pl.enter_context(nc.sbuf_tensor("st2b", [128, NSLOT], F32))
                st2c = pl.enter_context(nc.sbuf_tensor("st2c", [128, NSLOT], F32))
                st2d = pl.enter_context(nc.sbuf_tensor("st2d", [128, NSLOT], F32))
                st2e = pl.enter_context(nc.sbuf_tensor("st2e", [128, NSLOT], F32))
                st2f = pl.enter_context(nc.sbuf_tensor("st2f", [128, NSLOT], F32))
                ps_l = pl.enter_context(nc.psum_tensor("ps_l", [128, 1024], F32))
                ps_l2 = pl.enter_context(nc.psum_tensor("ps_l2", [128, 1024], F32))
                ps_lb = [ps_l, ps_l2]

                dve_x2 = 0
                for hf in range(2):
                    ts0 = hf * 4
                    plan.wait("sync", "st_s", st_attn)
                    plan.wait("sync", "ld_s", plan.val("ld_s"))
                    if hf == 1:
                        plan.wait("sync", "dve", dve_x2)
                    ld_r1 = plan.op_inc(
                        "sync", "ld_s", 16,
                        lambda e, ts0=ts0: e.dma_start(
                            out=r1sb.rearrange("p (t c) -> p t c", t=4),
                            in_=r1_t[:, ts0 : ts0 + 4, :],
                        ),
                    )
                    plan.wait("scalar", "ld_s", ld_r1)
                    for t in range(4):
                        plan.op(
                            "scalar",
                            lambda e, t=t, ts0=ts0: e.activation(
                                sq2[:], r1sb[:, t * DM : (t + 1) * DM], AF.Square,
                                accum_out=st2b[:, ts0 + t : ts0 + t + 1],
                            ),
                        )
                    for t in range(4):
                        plan.op(
                            "scalar",
                            lambda e, t=t, ts0=ts0: e.activation(
                                sq2[:], r1sb[:, t * DM : (t + 1) * DM], AF.Copy,
                                accum_out=st2[:, ts0 + t : ts0 + t + 1],
                            ),
                        )
                    cc = slice(ts0, ts0 + 4)
                    plan.op("scalar", lambda e, cc=cc: e.activation(st2[:, cc], st2[:, cc], AF.Copy, scale=-1.0 / DM))
                    a_m2 = plan.op_inc(
                        "scalar", "act", 1,
                        lambda e, cc=cc: e.activation(st2e[:, cc], st2[:, cc], AF.Square),
                    )
                    plan.wait("vector", "act", a_m2)
                    d_m2 = plan.op_inc(
                        "vector", "dve", 1,
                        lambda e, cc=cc: e.tensor_scalar(st2e[:, cc], st2e[:, cc], -1.0, EPS, ALU.mult, ALU.add),
                    )
                    plan.wait("scalar", "dve", d_m2)
                    plan.op("scalar", lambda e, cc=cc: e.activation(st2b[:, cc], st2b[:, cc], AF.Copy, scale=1.0 / DM))
                    act_sq2 = 0
                    for t in range(4):
                        ci = slice(ts0 + t, ts0 + t + 1)
                        plan.op("scalar", lambda e, ci=ci: e.activation(st2f[:, ci], st2b[:, ci], AF.Ln, bias=st2e[:, ci]))
                        fn = lambda e, ci=ci: e.activation(st2c[:, ci], st2f[:, ci], AF.Exp, scale=-0.5)
                        if t == 3:
                            act_sq2 = plan.op_inc("scalar", "act", 1, fn)
                        else:
                            plan.op("scalar", fn)
                    plan.wait("vector", "act", act_sq2)
                    for t in range(4):
                        fn = lambda e, t=t, ts0=ts0: e.tensor_scalar(
                            x2tm[:, (ts0 + t) * DM : (ts0 + t + 1) * DM],
                            r1sb[:, t * DM : (t + 1) * DM],
                            st2[:, ts0 + t : ts0 + t + 1], st2c[:, ts0 + t : ts0 + t + 1],
                            ALU.add, ALU.mult,
                        )
                        if t == 3:
                            dve_x2 = plan.op_inc("vector", "dve", 1, fn)
                        else:
                            plan.op("vector", fn)
                plan.wait("tensor", "dve", dve_x2)
                for _ in range(8):
                    plan.op(
                        "tensor",
                        lambda e: e.matmul(
                            ps_l[:, 0:128], ident_b[:, :], ident_b[:, :],
                            start=True, stop=True, skip_group_check=True,
                        ),
                    )
                act_ln2 = 0
                ev_ln2 = [0] * NHC
                for hc in range(NHC):
                    psl = ps_lb[hc % 2]
                    if hc >= 2:
                        plan.wait("tensor", "act", ev_ln2[hc - 2])
                    for t in range(NSLOT):
                        fn = lambda e, hc=hc, t=t, psl=psl: e.matmul(
                            psl[:, t * 128 : (t + 1) * 128],
                            x2tm[:, t * DM + hc * 128 : t * DM + (hc + 1) * 128],
                            ident_b[:, :],
                            start=True, stop=True, skip_group_check=True,
                        )
                        if t == NSLOT - 1:
                            v = plan.op_inc("tensor", "pe", 1, fn)
                        else:
                            plan.op("tensor", fn)
                    plan.wait("scalar", "pe", v)
                    act_ln2 = plan.op_inc(
                        "scalar", "act", 1,
                        lambda e, hc=hc, psl=psl: e.activation(
                            xn2_fm[:, hc * TPC : (hc + 1) * TPC], psl[:, :], AF.Copy
                        ),
                    )
                    ev_ln2[hc] = act_ln2

            wgt = [pg.enter_context(nc.sbuf_tensor(f"wgt{b}", [128, DM], BF16)) for b in range(2)]
            wut = [pg.enter_context(nc.sbuf_tensor(f"wut{b}", [128, DM], BF16)) for b in range(2)]
            gsil = pg.enter_context(nc.sbuf_tensor("gsil", [128, TPC], BF16))
            gate_ps = pg.enter_context(nc.psum_tensor("gate_ps", [128, 1024], F32))
            up_ps = pg.enter_context(nc.psum_tensor("up_ps", [128, 1024], F32))

            ib_mm = [0] * NIB
            ib_sil = [0] * NIB
            ib_ups = [0] * NIB

            for ib in range(NIB):
                b = ib % 2
                if ib < 2:
                    plan.wait("gpsimd", "act", act_ln2)
                else:
                    plan.wait("gpsimd", "pe", ib_mm[ib - 2])
                plan.op_inc(
                    "gpsimd", "ld_g", 16,
                    lambda e, ib=ib, b=b: e.dma_start(
                        out=wgt[b][:], in_=wg_in[ib * 128 : (ib + 1) * 128, :]
                    ),
                )
                ib_ld = plan.op_inc(
                    "gpsimd", "ld_g", 16,
                    lambda e, ib=ib, b=b: e.dma_start(
                        out=wut[b][:], in_=wu_in[ib * 128 : (ib + 1) * 128, :]
                    ),
                )
                plan.wait("tensor", "ld_g", ib_ld)
                if ib == 0:
                    plan.wait("tensor", "act", act_ln2)
                if ib >= 1:
                    plan.wait("tensor", "act", ib_sil[ib - 1])
                    plan.wait("tensor", "dve", ib_ups[ib - 1])
                for wsel in range(2):
                    wt = wgt[b] if wsel == 0 else wut[b]
                    ps = gate_ps if wsel == 0 else up_ps
                    for hc in range(NHC):
                        for half in range(2):
                            fn = lambda e, wt=wt, ps=ps, hc=hc, half=half: e.matmul(
                                ps[:, half * 512 : (half + 1) * 512],
                                wt[:, hc * 128 : (hc + 1) * 128],
                                xn2_fm[:, hc * TPC + half * 512 : hc * TPC + half * 512 + 512],
                                start=(hc == 0), stop=(hc == NHC - 1),
                                skip_group_check=True,
                            )
                            if hc == NHC - 1 and half == 1:
                                v = plan.op_inc("tensor", "pe", 1, fn)
                            else:
                                plan.op("tensor", fn)
                ib_mm[ib] = v
                plan.wait("scalar", "pe", ib_mm[ib])
                if ib >= 1:
                    plan.wait("scalar", "dve", ib_ups[ib - 1])
                ib_sil[ib] = plan.op_inc(
                    "scalar", "act", 1,
                    lambda e, ib=ib: e.activation(
                        gsil[:, :], gate_ps[:, :], AF.Silu, scale=gate_s_sb[:, ib : ib + 1]
                    ),
                )
                plan.wait("vector", "pe", ib_mm[ib])
                plan.wait("vector", "act", ib_sil[ib])
                ib_ups[ib] = plan.op_inc(
                    "vector", "dve", 1,
                    lambda e, ib=ib: e.scalar_tensor_tensor(
                        hdn[:, ib * TPC : (ib + 1) * TPC],
                        up_ps[:, :],
                        up_s_sb[:, ib : ib + 1],
                        gsil[:, :],
                        ALU.mult, ALU.mult,
                    ),
                )
            gate_end_pe = ib_mm[NIB - 1]
            gate_end_act = ib_sil[NIB - 1]
            gate_end_dve = ib_ups[NIB - 1]

        with ExitStack() as pd:
            wdt = [pd.enter_context(nc.sbuf_tensor(f"wdt{b}", [128, NIB * 128], BF16)) for b in range(2)]
            dsc = pd.enter_context(nc.sbuf_tensor("dsc", [128, TPC], F32))
            r1c = [pd.enter_context(nc.sbuf_tensor(f"r1c{b}", [128, TPC], F32)) for b in range(2)]
            outst = [pd.enter_context(nc.sbuf_tensor(f"outst{b}", [128, TPC], F32)) for b in range(2)]
            down_ps = pd.enter_context(nc.psum_tensor("down_ps", [128, 1024], F32))
            aux_ps = pd.enter_context(nc.psum_tensor("aux_ps", [128, 1024], F32))

            ht_mm = [0] * NHT
            ht_dsc = [0] * NHT
            ht_tm = [0] * NHT
            ht_add = [0] * NHT
            ht_st = [0] * NHT

            for ht in range(NHT):
                b = ht % 2
                if ht < 2:
                    plan.wait("gpsimd", "pe", gate_end_pe)
                    plan.wait("gpsimd", "dve", gate_end_dve)
                    plan.wait("gpsimd", "act", gate_end_act)
                else:
                    plan.wait("gpsimd", "pe", ht_mm[ht - 2])
                    plan.wait("gpsimd", "st_s", ht_st[ht - 2])
                plan.op_inc(
                    "gpsimd", "ld_g", 16,
                    lambda e, ht=ht, b=b: e.dma_start(
                        out=wdt[b][:], in_=wd_in[ht * 128 : (ht + 1) * 128, :]
                    ),
                )
                ht_ld = plan.op_inc(
                    "gpsimd", "ld_g", 16,
                    lambda e, ht=ht, b=b: e.dma_start(
                        out=r1c[b].rearrange("p (s c) -> p s c", s=NSLOT),
                        in_=r1_t[:, :, ht * 128 : (ht + 1) * 128],
                    ),
                )
                plan.wait("tensor", "ld_g", ht_ld)
                if ht == 0:
                    plan.wait("tensor", "act", gate_end_act)
                    plan.wait("tensor", "dve", gate_end_dve)
                if ht >= 1:
                    plan.wait("tensor", "act", ht_dsc[ht - 1])
                    plan.wait("tensor", "dve", ht_add[ht - 1])
                for ib in range(NIB):
                    for half in range(2):
                        fn = lambda e, b=b, ib=ib, half=half: e.matmul(
                            down_ps[:, half * 512 : (half + 1) * 512],
                            wdt[b][:, ib * 128 : (ib + 1) * 128],
                            hdn[:, ib * TPC + half * 512 : ib * TPC + half * 512 + 512],
                            start=(ib == 0), stop=(ib == NIB - 1),
                            skip_group_check=True,
                        )
                        if ib == NIB - 1 and half == 1:
                            v = plan.op_inc("tensor", "pe", 1, fn)
                        else:
                            plan.op("tensor", fn)
                ht_mm[ht] = v
                plan.wait("scalar", "pe", ht_mm[ht])
                if ht >= 1:
                    plan.wait("scalar", "pe", ht_tm[ht - 1])
                ht_dsc[ht] = plan.op_inc(
                    "scalar", "act", 1,
                    lambda e, ht=ht: e.activation(
                        dsc[:, :], down_ps[:, :], AF.Copy, scale=down_s_sb[:, ht : ht + 1]
                    ),
                )
                plan.wait("tensor", "act", ht_dsc[ht])
                for ts in range(NSLOT):
                    fn = lambda e, ts=ts: e.matmul(
                        aux_ps[:, ts * 128 : (ts + 1) * 128],
                        dsc[:, ts * 128 : (ts + 1) * 128],
                        ident_f[:, :],
                        start=True, stop=True, skip_group_check=True,
                    )
                    if ts == NSLOT - 1:
                        ht_tm[ht] = plan.op_inc("tensor", "pe", 1, fn)
                    else:
                        plan.op("tensor", fn)
                plan.wait("vector", "pe", ht_tm[ht])
                plan.wait("vector", "ld_g", ht_ld)
                if ht >= 2:
                    plan.wait("vector", "st_s", ht_st[ht - 2])
                ht_add[ht] = plan.op_inc(
                    "vector", "dve", 1,
                    lambda e, b=b: e.tensor_tensor(
                        outst[b][:, :], aux_ps[:, :], r1c[b][:, :], ALU.add
                    ),
                )
                plan.wait("sync", "dve", ht_add[ht])
                ht_st[ht] = plan.op_inc(
                    "sync", "st_s", 16,
                    lambda e, ht=ht, b=b: e.dma_start(
                        out=out_t[:, :, ht * 128 : (ht + 1) * 128],
                        in_=outst[b].rearrange("p (s c) -> p s c", s=NSLOT),
                    ),
                )

        if dump_xno:
            plan.wait("gpsimd", "st_s", plan.val("st_s"))
            plan.wait("gpsimd", "st_a", plan.val("st_a"))
            v_x2 = plan.op_inc(
                "gpsimd", "st_g", 16,
                lambda e: e.dma_start(out=dbg_xno2[:, :], in_=xn_own[:, :]),
            )
        plan.wait("sync", "st_s", plan.val("st_s"))
        plan.wait("sync", "st_a", plan.val("st_a"))
        plan.wait("sync", "st_g", plan.val("st_g"))

        with nc.Block() as block:
            @block.sync
            def _(e):
                for fn in plan.ops["sync"]:
                    fn(e, sems)

            @block.gpsimd
            def _(e):
                for fn in plan.ops["gpsimd"]:
                    fn(e, sems)

            @block.tensor
            def _(e):
                for fn in plan.ops["tensor"]:
                    fn(e, sems)

            @block.scalar
            def _(e):
                for fn in plan.ops["scalar"]:
                    fn(e, sems)

            @block.vector
            def _(e):
                for fn in plan.ops["vector"]:
                    fn(e, sems)

    return nc


def _host_prep(L0_attn_x, gate_w, up_w, down_w, gate_s, up_s, down_s):
    x = np.ascontiguousarray(np.asarray(L0_attn_x, dtype=np.float32).reshape(S, DM))
    gate_w = np.ascontiguousarray(
        np.asarray(gate_w, dtype=np.float32)
        .reshape(NIB, 128, NHC, 128).transpose(0, 3, 2, 1).reshape(INTER, DM)
    )
    up_w = np.ascontiguousarray(
        np.asarray(up_w, dtype=np.float32)
        .reshape(NIB, 128, NHC, 128).transpose(0, 3, 2, 1).reshape(INTER, DM)
    )
    down_w = np.ascontiguousarray(
        np.asarray(down_w, dtype=np.float32)
        .reshape(NHT, 128, NIB, 128).transpose(0, 3, 2, 1).reshape(DM, INTER)
    )
    gs = np.ascontiguousarray(np.asarray(gate_s, dtype=np.float32).reshape(NIB, 128))
    us = np.ascontiguousarray(np.asarray(up_s, dtype=np.float32).reshape(NIB, 128))
    ds = np.ascontiguousarray(np.asarray(down_s, dtype=np.float32).reshape(NHT, 128))
    idb = np.eye(128, dtype=ml_dtypes.bfloat16)
    idf = np.eye(128, dtype=np.float32)

    tri = np.tril(np.ones((128, 128), dtype=np.float32)).T.astype(ml_dtypes.bfloat16)
    onesm = np.ones((128, 128), dtype=ml_dtypes.bfloat16)
    zerom = np.zeros((128, 128), dtype=ml_dtypes.bfloat16)

    in_maps = []
    for c in range(N_CORES):
        rowsel = [x[slot_qtile(s_, c) * 128 : (slot_qtile(s_, c) + 1) * 128] for s_ in range(NSLOT)]
        x_own = np.ascontiguousarray(np.concatenate(rowsel, axis=0))
        m = np.empty((NKT * 128, 128), dtype=ml_dtypes.bfloat16)
        for kt in range(NKT):
            s_star = (63 - kt) // 8
            d = kt - slot_qtile(s_star, c)
            blk = onesm if d < 0 else (tri if d == 0 else zerom)
            m[kt * 128 : (kt + 1) * 128] = blk
        in_maps.append(
            {
                "x_full": x,
                "x_own": x_own,
                "masks": m,
                "wg": gate_w,
                "wu": up_w,
                "wd": down_w,
                "gs": gs,
                "us": us,
                "ds": ds,
                "idb": idb,
                "idf": idf,
            }
        )
    return in_maps


def kernel(
    L0_attn_x,
    L0_ffn_gate_w,
    L0_ffn_up_w,
    L0_ffn_down_w,
    L0_ffn_gate_s,
    L0_ffn_up_s,
    L0_ffn_down_s,
):
    in_maps = _host_prep(
        L0_attn_x, L0_ffn_gate_w, L0_ffn_up_w, L0_ffn_down_w,
        L0_ffn_gate_s, L0_ffn_up_s, L0_ffn_down_s,
    )
    nc = build_program()
    res = run_bass_kernel_spmd(nc, in_maps, core_ids=list(range(N_CORES)))
    out = np.empty((S, DM), dtype=np.float32)
    for c in range(N_CORES):
        oc = res.results[c]["out"]
        for s_ in range(NSLOT):
            j = slot_qtile(s_, c)
            out[j * 128 : (j + 1) * 128] = oc[s_ * 128 : (s_ + 1) * 128]
    return out.reshape(1, S, DM)
